# version 06 (final)
# baseline (speedup 1.0000x reference)
from collections import defaultdict

ENGINES = ("tensor", "vector", "scalar", "gpsimd", "sync")
DMA_POOL = 10


class _Op:
    __slots__ = ("engine", "fn", "reads", "writes", "idx", "is_dma",
                 "waits", "tag", "no_edge_from_prev", "dma_sem", "war_regions",
                 "dma_group")

    def __init__(self, engine, fn, reads, writes, idx, is_dma, tag,
                 no_edge_from_prev=False):
        self.engine = engine
        self.fn = fn
        self.reads = reads
        self.writes = writes
        self.idx = idx
        self.is_dma = is_dma
        self.waits = []
        self.tag = tag
        self.no_edge_from_prev = no_edge_from_prev
        self.dma_sem = None
        self.war_regions = []


class Emitter:
    def __init__(self, nc, dma_pool=DMA_POOL):
        self.nc = nc
        self.dma_pool = dma_pool
        self.ops = []
        self.eng_count = defaultdict(int)
        self.group_sem = {}
        self.group_last = {}
        self.group_closed = set()
        self.sem_counts = defaultdict(int)
        self.sem_rotation = defaultdict(int)
        self.sem_tenant = {}
        self.last_write = {}
        self.readers = defaultdict(list)

    def op(self, engine, fn, reads=(), writes=(), tag="", accum=False):
        self.eng_count[engine] += 1
        o = _Op(engine, fn, list(reads), list(writes), self.eng_count[engine],
                False, tag, no_edge_from_prev=accum)
        self._resolve(o)
        self.ops.append(o)
        return o

    def dma(self, engine, fn, reads=(), writes=(), group=None, tag=""):
        if group is None:
            group = ("_anon", len(self.ops))
        group = (engine, group)
        if group in self.group_sem:
            assert group not in self.group_closed, \
                f"dma group {group} already consumed; cannot append more DMAs"
            qs = self.group_sem[group]
            new_group = False
        else:
            slot = self.sem_rotation[engine] % self.dma_pool
            self.sem_rotation[engine] += 1
            qs = (engine, slot)
            old = self.sem_tenant.get(qs)
            if old is not None:
                self.group_closed.add(old)
            self.group_sem[group] = qs
            self.sem_tenant[qs] = group
            new_group = True
        o = _Op(engine, fn, list(reads), list(writes), self.sem_counts[qs] + 16,
                True, tag)
        o.dma_sem = qs
        o.dma_group = group
        if new_group:
            o.war_regions.append(("dmasem",) + qs)
        self._resolve(o)
        self.sem_counts[qs] += 16
        self.group_last[group] = self.sem_counts[qs]
        self.ops.append(o)
        return o

    def adopt(self, new_region, old_regions):
        for old in old_regions:
            w = self.last_write.get(old)
            if w is not None:
                self.readers[new_region].append(w)
            self.readers[new_region].extend(self.readers.get(old, ()))

    def _resolve(self, o):
        me = ("eng", o.engine) if not o.is_dma else ("dma", o.dma_sem)
        need = {}

        def add_need(stamp, waiter=o):
            kind, key, cnt = stamp
            if kind == "eng" and (kind, key) == me and o.no_edge_from_prev:
                return
            k = (kind, key)
            need[k] = max(need.get(k, 0), cnt)

        regions = []
        for r in o.reads:
            w = self.last_write.get(r)
            if w:
                add_need(w)
        for r in list(o.writes) + list(o.war_regions):
            w = self.last_write.get(r)
            if w:
                add_need(w)
            for rd in self.readers.get(r, ()):
                add_need(rd)

        fixed = []
        for (kind, key), cnt in need.items():
            if kind == "dma":
                slot, grp = key
                self.group_closed.add(grp)
                cnt = self.group_last[grp]
                key = slot
                mykind = "dma" if o.is_dma else "eng"
                mykey = o.dma_sem if o.is_dma else o.engine
                self.readers[("dmasem",) + slot].append(
                    (mykind, (mykey, getattr(o, "dma_group", None)) if o.is_dma
                     else mykey, o.idx))
            fixed.append((kind, key, cnt))
        o.waits = fixed

        mykind = "dma" if o.is_dma else "eng"
        mykey = (o.dma_sem, o.dma_group) if o.is_dma else o.engine
        stamp = (mykind, mykey, o.idx)
        for r in list(o.writes) + list(o.war_regions):
            self.last_write[r] = stamp
            self.readers[r] = []
        for r in o.reads:
            self.readers[r].append(stamp)

    def finalize(self, block):
        nc = self.nc
        needed = set()
        for o in self.ops:
            for kind, key, cnt in o.waits:
                if kind == "eng":
                    needed.add((key, cnt))
        inc_idxs = defaultdict(list)
        for eng, cnt in needed:
            inc_idxs[eng].append(cnt)
        for eng in inc_idxs:
            inc_idxs[eng] = sorted(set(inc_idxs[eng]))
        semval_of_idx = {}
        for eng, idxs in inc_idxs.items():
            for v, idx in enumerate(idxs):
                semval_of_idx[(eng, idx)] = v + 1

        sems = {}
        for eng in ENGINES:
            if inc_idxs.get(eng):
                sems[eng] = nc.semaphore(f"s_{eng}").__enter__()
        dsems = {}
        for qs in self.sem_counts:
            dsems[qs] = nc.semaphore(f"d_{qs[0]}_{qs[1]}").__enter__()

        per_engine = defaultdict(list)
        for o in self.ops:
            per_engine[o.engine].append(o)
        mark = {(eng, idx) for eng, idxs in inc_idxs.items() for idx in idxs}

        def wait_value_for(eng2, cnt):
            idxs = inc_idxs[eng2]
            import bisect
            i = bisect.bisect_left(idxs, cnt)
            return semval_of_idx[(eng2, idxs[i])]

        for eng, ops in per_engine.items():
            def make_section(eng=eng, ops=ops):
                def section(handle):
                    waited = defaultdict(int)
                    for o in ops:
                        for kind, key, cnt in o.waits:
                            if kind == "eng":
                                val = wait_value_for(key, cnt)
                                if waited[("eng", key)] < val:
                                    handle.wait_ge(sems[key], val)
                                    waited[("eng", key)] = val
                            else:
                                if waited[("dma", key)] < cnt:
                                    handle.wait_ge(dsems[key], cnt)
                                    waited[("dma", key)] = cnt
                        inst = o.fn(handle)
                        if o.is_dma:
                            inst.then_inc(dsems[o.dma_sem], 16)
                        elif (eng, o.idx) in mark:
                            inst.then_inc(sems[eng], 1)
                return section
            getattr(block, eng)(make_section())

        return sems, dsems

    def final_gate(self, engine, regions):
        return self.op(engine, lambda h: h.nop(), reads=list(regions),
                       writes=[], tag="final")


"""mk_sandwich_d2 megakernel builders (parameterized for small-shape testing).

Conventions:
  FM (feature-major): SBUF [feature-chunk-of-128 partitions, tokens free]
  TM (token-major):   SBUF [token-chunk-of-128 partitions, features free]
  All matmuls bf16 operands, fp32 PSUM. Residual fp32 FM.
  matmul(out, lhsT, rhs): out[M,N] = sum_K lhsT[K,M] * rhs[K,N], K = partitions.

PSUM: 8 banks, allocated once as ctx.bank[0..6] ([128,512] f32) and
ctx.bankT ([128,1024] bf16, for PE transposes). Matmul outs are slices.
Region discipline: every op declares reads/writes region keys; PSUM bank i
uses region ("pb", i) (sub-slicing not tracked -> treat a bank as one region).

Host passes weights pre-transposed as needed (pure layout).
"""
import numpy as np
import ml_dtypes
import concourse.bass as bass
import concourse.mybir as mybir

F32 = mybir.dt.float32
BF16 = mybir.dt.bfloat16
AF = mybir.ActivationFunctionType
ALU = mybir.AluOpType
AX = mybir.AxisListType.X

BF = ml_dtypes.bfloat16
NMAX = 512


def cdiv(a, b):
    return (a + b - 1) // b


class Ctx:

    def __init__(self, nc, em, stack, n_f32_banks=7, transpose_bank=True):
        self.nc = nc
        self.em = em
        self.stack = stack
        self.uid = 0
        self._const = {}
        self.bank = [stack.enter_context(nc.psum_tensor(f"pbank{i}", [128, 512], F32))
                     for i in range(n_f32_banks)]
        self.bankT = (stack.enter_context(nc.psum_tensor("pbankT", [128, 1024], BF16))
                      if transpose_bank else None)

    def sb(self, name, shape, dtype):
        return self.stack.enter_context(self.nc.sbuf_tensor(name, list(shape), dtype))

    def name(self, base):
        self.uid += 1
        return f"{base}{self.uid}"

    def ones_col(self, p=128, dtype=BF16):
        key = ("ones_col", p) if dtype is BF16 else ("ones_col", p, str(dtype))
        if key not in self._const:
            t = self.sb(f"ones_col{p}_{len(self._const)}", [p, 1], dtype)
            self.em.op("gpsimd", lambda g: g.memset(t[:, :], 1.0), writes=[key])
            self._const[key] = t
        return self._const[key]

    def ones_row(self, n=128, dtype=BF16):
        key = ("ones_row", n, str(dtype))
        if key not in self._const:
            t = self.sb(f"ones_row{n}_{str(dtype)[-4:]}", [1, n], dtype)
            self.em.op("gpsimd", lambda g: g.memset(t[:, :], 1.0), writes=[key])
            self._const[key] = t
        return self._const[key]

    def identity(self, p=128):
        key = ("ident", p)
        if key not in self._const:
            t = self.sb(f"ident{p}", [p, p], BF16)
            self.em.op("gpsimd", lambda g: g.iota(t[:, :], [[1, p]], channel_multiplier=-1,
                                                  allow_small_or_imprecise_dtypes=True),
                       writes=[key])
            self.em.op("vector", lambda v: v.tensor_single_scalar(t[:, :], t[:, :], 0.0,
                                                                  ALU.is_equal),
                       reads=[key], writes=[key])
            self._const[key] = t
        return self._const[key]


def pb(i):
    return ("pb", i)


def bcast_row(ctx, src_region, src_ap, dst_sb, dst_region, n_part, ncols, bank_i):
    em = ctx.em
    dt = src_ap.tensor.dtype if hasattr(src_ap, "tensor") else F32
    ones = ctx.ones_row(n_part, dt)
    okey = ("ones_row", n_part, str(dt))
    bankt = ctx.bank[bank_i]
    for j0 in range(0, ncols, NMAX):
        j1 = min(j0 + NMAX, ncols)
        em.op("tensor", lambda t, j0=j0, j1=j1: t.matmul(
            bankt[0:n_part, 0:j1 - j0], ones[:, 0:n_part], src_ap[0:1, j0:j1],
            start=True, stop=True),
            reads=[src_region, okey], writes=[pb(bank_i)])
        em.op("vector", lambda v, j0=j0, j1=j1: v.tensor_copy(
            dst_sb[0:n_part, j0:j1], bankt[0:n_part, 0:j1 - j0]),
            reads=[pb(bank_i)], writes=[dst_region])


def matmul_acc(ctx, out_ap, out_region, lhs_list, rhs_list, bank_tile, bank_region,
               evac, n0, n1, extra_reads=()):
    em = ctx.em
    K = len(lhs_list)
    for i, ((la, lr), (ra, rr)) in enumerate(zip(lhs_list, rhs_list)):
        em.op("tensor", lambda t, la=la, ra=ra, i=i: t.matmul(
            bank_tile, la, ra, start=(i == 0), stop=(i == K - 1)),
            reads=[lr, rr], writes=[bank_region], accum=(i > 0))
    evac()


def ln_fm(ctx, x_chunks, x_region, T, D, out_chunks, out_region,
          eps=1e-5, gamma_beta=None, tag="ln", banks=(0, 1), x_f32=False,
          col0=0):
    em = ctx.em
    nD = D // 128
    name = ctx.name(tag)
    if not hasattr(ctx, "_ln_scratch"):
        srow = ctx.sb("lns_srow", [1, NMAX], F32)
        qrow = ctx.sb("lns_qrow", [1, NMAX], F32)
        msq = ctx.sb("lns_msq", [1, NMAX], F32)
        mb = ctx.sb("lns_mb", [128, NMAX], BF16)
        rb = ctx.sb("lns_rb", [128, NMAX], BF16)
        sq = ctx.sb("lns_sq", [128, NMAX], BF16)
        tmp = ctx.sb("lns_tmp", [128, NMAX], BF16)
        ctx._ln_scratch = (srow, qrow, msq, mb, rb, sq, tmp)
    srow, qrow, msq, mb, rb, sq, tmp = ctx._ln_scratch
    R = lambda s: "lnscr" + s
    ones = ctx.ones_col(128, F32 if x_f32 else BF16)
    okey = ("ones_col", 128) if not x_f32 else ("ones_col", 128, str(F32))
    onesb = ctx.ones_col(128)
    b0, b1 = ctx.bank[banks[0]], ctx.bank[banks[1]]

    for j0 in range(0, T, NMAX):
        j1 = min(j0 + NMAX, T)
        w = j1 - j0
        a0, a1 = col0 + j0, col0 + j1
        for c in range(nD):
            em.op("tensor", lambda t, c=c, a0=a0, a1=a1, w=w: t.matmul(
                b0[0:1, 0:w], ones[:, :], x_chunks[c][:, a0:a1],
                start=(c == 0), stop=(c == nD - 1)),
                reads=[(x_region, c), okey], writes=[pb(banks[0])],
                accum=(c > 0))
        em.op("vector", lambda v, w=w: v.tensor_scalar_mul(
            srow[0:1, 0:w], b0[0:1, 0:w], 1.0 / D),
            reads=[pb(banks[0])], writes=[R("s")])
        for c in range(nD):
            em.op("scalar", lambda s, c=c, a0=a0, a1=a1, w=w: s.activation(
                sq[:, 0:w], x_chunks[c][:, a0:a1], AF.Square),
                reads=[(x_region, c)], writes=[R("sq")])
            em.op("tensor", lambda t, c=c, w=w: t.matmul(
                b1[0:1, 0:w], onesb[:, :], sq[:, 0:w],
                start=(c == 0), stop=(c == nD - 1)),
                reads=[R("sq"), ("ones_col", 128)], writes=[pb(banks[1])],
                accum=(c > 0))
        em.op("scalar", lambda s, w=w: s.activation(msq[0:1, 0:w], srow[0:1, 0:w],
                                                    AF.Square),
              reads=[R("s")], writes=[R("msq")])
        em.op("vector", lambda v, w=w: v.scalar_tensor_tensor(
            qrow[0:1, 0:w], b1[0:1, 0:w], 1.0 / D, msq[0:1, 0:w],
            ALU.mult, ALU.subtract),
            reads=[pb(banks[1]), R("msq")], writes=[R("q")])
        em.op("vector", lambda v, w=w: v.tensor_single_scalar(
            qrow[0:1, 0:w], qrow[0:1, 0:w], float(eps), ALU.add),
            reads=[R("q")], writes=[R("q")])
        em.op("scalar", lambda s, w=w: s.activation(qrow[0:1, 0:w], qrow[0:1, 0:w],
                                                    AF.Ln),
              reads=[R("q")], writes=[R("q")])
        em.op("vector", lambda v, w=w: v.tensor_scalar_mul(
            qrow[0:1, 0:w], qrow[0:1, 0:w], -0.5),
            reads=[R("q")], writes=[R("q")])
        em.op("scalar", lambda s, w=w: s.activation(qrow[0:1, 0:w], qrow[0:1, 0:w],
                                                    AF.Exp),
              reads=[R("q")], writes=[R("q")])
        bcast_row(ctx, R("s"), srow[0:1, 0:w], mb, R("mb"), 128, w, banks[0])
        bcast_row(ctx, R("q"), qrow[0:1, 0:w], rb, R("rb"), 128, w, banks[1])
        for c in range(nD):
            em.op("vector", lambda v, c=c, a0=a0, a1=a1, w=w: v.tensor_sub(
                tmp[:, 0:w], x_chunks[c][:, a0:a1], mb[:, 0:w]),
                reads=[(x_region, c), R("mb")], writes=[R("tmp")])
            if gamma_beta is None:
                em.op("vector", lambda v, c=c, a0=a0, a1=a1, w=w: v.tensor_mul(
                    out_chunks[c][:, a0:a1], tmp[:, 0:w], rb[:, 0:w]),
                    reads=[R("tmp"), R("rb")], writes=[(out_region, c)])
            else:
                gcols, bcols, gbreg = gamma_beta
                gbr = list(gbreg) if isinstance(gbreg, (list, tuple)) else [gbreg]
                em.op("vector", lambda v, c=c, w=w: v.tensor_mul(
                    tmp[:, 0:w], tmp[:, 0:w], rb[:, 0:w]),
                    reads=[R("tmp"), R("rb")], writes=[R("tmp")])
                em.op("vector", lambda v, c=c, a0=a0, a1=a1, w=w: v.tensor_scalar(
                    out_chunks[c][:, a0:a1], tmp[:, 0:w], gcols[c][:, :],
                    bcols[c][:, :], ALU.mult, ALU.add),
                    reads=[R("tmp")] + gbr, writes=[(out_region, c)])
    return name



def load_fm(ctx, src_dram, src_region, T, D, scratch_dram, fm_chunks, fm_region,
            group):
    em = ctx.em
    nD = D // 128
    CR = 512
    for r0 in range(0, T, CR):
        r1 = min(r0 + CR, T)
        em.dma("gpsimd", lambda g, r0=r0, r1=r1: g.dma_start(
            out=scratch_dram[r0:r1, :], in_=src_dram[r0:r1, :]),
            reads=[src_region], writes=[(fm_region, "scr", r0)], group=group + "_cast")
    scr_regions = [(fm_region, "scr", r0) for r0 in range(0, T, CR)]
    for c in range(nD):
        em.dma("sync", lambda s, c=c: s.dma_start_transpose(
            fm_chunks[c][:, 0:T], scratch_dram[0:T, c * 128:(c + 1) * 128]),
            reads=scr_regions, writes=[(fm_region, c)], group=group + "_tr")


ATT_DEBUG = {"skip_norms": False, "skip_att": False, "skip_store": False}


def attn_inner(ctx, S, HPC, DK, xq_fm, xq_region, Dq,
               xk_fm, xk_region, Dk,
               wq_sb, wq_region, wk_sb, wk_region, wv_sb, wv_region,
               o_dram, o_region, o_group, scale, tag="att", qkv_bias=None):
    em = ctx.em
    name = ctx.name(tag)
    R = lambda s: name + s
    nq = Dq // 128
    nk = Dk // 128
    npair = HPC // 2
    ntile = S // 128
    VW = 65
    q2 = ctx.sb(name + "_q2", [128, S], BF16)
    k2 = ctx.sb(name + "_k2", [128, S], BF16)
    vtm = ctx.sb(name + "_vtm", [128, ntile * HPC * VW], BF16)
    sq = ctx.sb(name + "_sq", [128, NMAX], BF16)
    n2 = ctx.sb(name + "_n2", [2, S], F32)
    mx = ctx.sb(name + "_mx", [2, 4], F32)
    negc = ctx.sb(name + "_negc", [128, 2], F32)
    mxb = ctx.sb(name + "_mxb", [2, 1], BF16)
    mxr = ctx.sb(name + "_mxr", [1, 2], F32)
    bcols = ctx.sb(name + "_bcols", [128, 2], F32)
    pt = ctx.sb(name + "_pt", [128, NMAX], BF16)
    rb = ctx.sb(name + "_rb", [64, NMAX], BF16)
    rr = ctx.sb(name + "_rr", [1, NMAX], F32)
    ob = ctx.sb(name + "_ob", [64, NMAX], BF16)
    ones2 = ctx.sb(name + "_ones2", [128, 2], BF16)
    em.op("gpsimd", lambda g: g.memset(ones2[:, :], 0.0), writes=[R("ones2")])
    em.op("gpsimd", lambda g: g.memset(ones2[0:64, 0:1], 1.0),
          reads=[R("ones2")], writes=[R("ones2")])
    em.op("gpsimd", lambda g: g.memset(ones2[64:128, 1:2], 1.0),
          reads=[R("ones2")], writes=[R("ones2")])

    ratio = NMAX // 128
    masks = []
    for s in range(ratio):
        m = ctx.sb(name + f"_mask{s}", [128, NMAX], BF16)
        em.op("gpsimd", lambda g, m=m, s=s: g.iota(
            m[:, :], [[1, NMAX]], base=-128 * s, channel_multiplier=-1,
            allow_small_or_imprecise_dtypes=True), writes=[(R("mask"), s)])
        em.op("vector", lambda v, m=m, s=s: v.tensor_single_scalar(
            m[:, :], m[:, :], 0.0, ALU.is_ge),
            reads=[(R("mask"), s)], writes=[(R("mask"), s)])
        masks.append(m)

    em.op("gpsimd", lambda g: g.memset(vtm[:, :], 1.0), writes=[R("vtm_init")])
    for tt in range(ntile):
        base = tt * HPC * VW
        for c in range(nk):
            em.op("tensor", lambda t, tt=tt, c=c: t.matmul(
                ctx.bank[0][:, 0:HPC * DK],
                xk_fm[c][:, tt * 128:(tt + 1) * 128], wv_sb[c][:, :],
                start=(c == 0), stop=(c == nk - 1)),
                reads=[(xk_region, c), (wv_region, c)], writes=[pb(0)], accum=(c > 0))
        for h in range(HPC):
            if qkv_bias is None:
                em.op("vector", lambda v, tt=tt, h=h, base=base: v.tensor_copy(
                    vtm[:, base + h * VW: base + h * VW + DK],
                    ctx.bank[0][:, h * DK:(h + 1) * DK]),
                    reads=[pb(0), R("vtm_init")], writes=[(R("vtm"), tt)])
            else:
                vb_ap, vb_reg = qkv_bias[4], qkv_bias[5]
                em.op("vector", lambda v, tt=tt, h=h, base=base, vb_ap=vb_ap: v.tensor_add(
                    vtm[:, base + h * VW: base + h * VW + DK],
                    ctx.bank[0][:, h * DK:(h + 1) * DK],
                    vb_ap[:, h * DK:(h + 1) * DK]),
                    reads=[pb(0), R("vtm_init"), qkv_bias[5]], writes=[(R("vtm"), tt)])

    for p in range(npair):
        if qkv_bias is not None:
            qbr_ap, qbr_reg, kbr_ap, kbr_reg = qkv_bias[0], qkv_bias[1], qkv_bias[2], qkv_bias[3]
            ones11 = ctx.ones_row(1, F32)
            for (brow_ap, brow_reg, col_idx) in ((qbr_ap, qbr_reg, 0), (kbr_ap, kbr_reg, 1)):
                em.op("tensor", lambda t, brow_ap=brow_ap, col_idx=col_idx, p=p: t.matmul(
                    ctx.bank[5][:, col_idx:col_idx + 1],
                    brow_ap[0:1, p * 128:(p + 1) * 128], ones11[:, :],
                    start=True, stop=True),
                    reads=[brow_reg, ("ones_row", 1, str(F32))], writes=[pb(5)])
            em.op("vector", lambda v: v.tensor_copy(bcols[:, 0:2], ctx.bank[5][:, 0:2]),
                  reads=[pb(5)], writes=[R("bcols")])
        for (dst, w_sb, w_region, x_fm, x_region, nC, scl, bci) in (
                (q2, wq_sb, wq_region, xq_fm, xq_region, nq, scale, 0),
                (k2, wk_sb, wk_region, xk_fm, xk_region, nk, 1.0, 1)):
            dreg = R("q2") if dst is q2 else R("k2")
            for j0 in range(0, S, NMAX):
                j1 = min(j0 + NMAX, S)
                w = j1 - j0
                for c in range(nC):
                    em.op("tensor", lambda t, c=c, j0=j0, w=w, w_sb=w_sb, p=p, x_fm=x_fm, nC=nC: t.matmul(
                        ctx.bank[1][:, 0:w], w_sb[c][:, p * 128:(p + 1) * 128],
                        x_fm[c][:, j0:j0 + w],
                        start=(c == 0), stop=(c == nC - 1)),
                        reads=[(x_region, c), (w_region, c)], writes=[pb(1)], accum=(c > 0))
                if qkv_bias is not None:
                    em.op("vector", lambda v, j0=j0, w=w, dst=dst, scl=scl, bci=bci: v.tensor_scalar(
                        dst[:, j0:j0 + w], ctx.bank[1][:, 0:w], bcols[:, bci:bci + 1],
                        float(scl), ALU.add, ALU.mult),
                        reads=[pb(1), R("bcols")], writes=[dreg])
                elif scl == 1.0:
                    em.op("vector", lambda v, j0=j0, w=w, dst=dst: v.tensor_copy(
                        dst[:, j0:j0 + w], ctx.bank[1][:, 0:w]),
                        reads=[pb(1)], writes=[dreg])
                else:
                    em.op("vector", lambda v, j0=j0, w=w, dst=dst, scl=scl: v.tensor_scalar_mul(
                        dst[:, j0:j0 + w], ctx.bank[1][:, 0:w], float(scl)),
                        reads=[pb(1)], writes=[dreg])

        sn = ATT_DEBUG["skip_norms"]
        if sn is True:
            em.op("gpsimd", lambda g: g.memset(negc[:, :], -30.0),
                  writes=[(R("negc"), 0), (R("negc"), 1)])
        else:
            o64 = ctx.ones_col(64)
            for hh in range(2):
                for (src, sreg, col) in ((q2, R("q2"), 0), (k2, R("k2"), 1)):
                    for j0 in range(0, S, NMAX):
                        j1 = min(j0 + NMAX, S)
                        w = j1 - j0
                        em.op("scalar", lambda s, j0=j0, w=w, src=src, hh=hh: s.activation(
                            sq[0:64, 0:w], src[hh * 64:(hh + 1) * 64, j0:j0 + w],
                            AF.Square),
                            reads=[sreg], writes=[R("sq")])
                        em.op("tensor", lambda t, j0=j0, w=w: t.matmul(
                            ctx.bank[1][0:1, 0:w], o64[:, :],
                            sq[0:64, 0:w], start=True, stop=True),
                            reads=[R("sq"), ("ones_col", 64)], writes=[pb(1)])
                        em.op("vector", lambda v, j0=j0, w=w: v.tensor_copy(
                            n2[0:1, j0:j0 + w], ctx.bank[1][0:1, 0:w]),
                            reads=[pb(1)], writes=[R("n2")])
                    em.op("vector", lambda v, col=col: v.reduce_max(
                        mx[0:1, col:col + 1], n2[0:1, 0:S], axis=AX),
                        reads=[R("n2")], writes=[(R("mx"), col)])
                em.op("vector", lambda v: v.tensor_mul(mx[0:1, 2:3], mx[0:1, 0:1],
                                                       mx[0:1, 1:2]),
                      reads=[(R("mx"), 0), (R("mx"), 1)], writes=[(R("mx"), 2)])
                em.op("scalar", lambda s: s.activation(mx[0:1, 2:3], mx[0:1, 2:3],
                                                       AF.Sqrt),
                      reads=[(R("mx"), 2)], writes=[(R("mx"), 2)])
                em.op("vector", lambda v: v.tensor_scalar(
                    mx[0:1, 3:4], mx[0:1, 2:3], -1.0, 40.0, ALU.mult, ALU.add),
                    reads=[(R("mx"), 2)], writes=[(R("mx"), 3)])
                bcast_row(ctx, (R("mx"), 3), mx[0:1, 3:4], negc[:, hh:hh + 1],
                          (R("negc"), hh), 128, 1, 1)

        for hh in range(2):
            h = p * 2 + hh
            hbase = hh * 64
            for j0 in range(0, S, NMAX):
                j1 = min(j0 + NMAX, S)
                w = j1 - j0
                ntk = (j1 - 1) // 128 + 1
                pvreg = pb(4)
                for tkc in range(ntk):
                    streg = (R("st"), h, j0, tkc)
                    sb_ = 2 + (tkc % 2)
                    em.op("tensor", lambda t, tkc=tkc, j0=j0, w=w, hbase=hbase, sb_=sb_: t.matmul(
                        ctx.bank[sb_][0:128, 0:w],
                        k2[hbase:hbase + 64, tkc * 128:(tkc + 1) * 128],
                        q2[hbase:hbase + 64, j0:j0 + w],
                        start=True, stop=True),
                        reads=[R("k2"), R("q2")], writes=[pb(sb_)])
                    em.op("scalar", lambda s, tkc=tkc, w=w, hh=hh, sb_=sb_: s.activation(
                        pt[:, 0:w], ctx.bank[sb_][:, 0:w], AF.Exp,
                        bias=negc[:, hh:hh + 1]),
                        reads=[pb(sb_), (R("negc"), hh)], writes=[R("pt")])
                    off = tkc * 128 - j0
                    if off >= 0:
                        s_idx = off // 128
                        em.op("vector", lambda v, w=w, s_idx=s_idx: v.tensor_mul(
                            pt[:, 0:w], pt[:, 0:w], masks[s_idx][:, 0:w]),
                            reads=[R("pt"), (R("mask"), s_idx)], writes=[R("pt")])
                    em.op("tensor", lambda t, tkc=tkc, w=w, h=h, ntk=ntk: t.matmul(
                        ctx.bank[4][0:VW, 0:w],
                        vtm[:, tkc * HPC * VW + h * VW: tkc * HPC * VW + (h + 1) * VW],
                        pt[:, 0:w],
                        start=(tkc == 0), stop=(tkc == ntk - 1)),
                        reads=[(R("vtm"), tkc), R("pt")], writes=[pvreg],
                        accum=(tkc > 0))
                em.op("vector", lambda v, w=w: v.reciprocal(rr[0:1, 0:w],
                                                            ctx.bank[4][64:65, 0:w]),
                      reads=[pvreg], writes=[R("rr")])
                bcast_row(ctx, R("rr"), rr, rb, R("rbb"), 64, w, 5)
                em.op("vector", lambda v, w=w: v.tensor_mul(
                    ob[0:64, 0:w], ctx.bank[4][0:64, 0:w], rb[0:64, 0:w]),
                    reads=[pvreg, R("rbb")], writes=[R("ob")])
                em.dma("sync", lambda s, h=h, j0=j0, w=w: s.dma_start(
                    out=o_dram[h * 64:(h + 1) * 64, j0:j0 + w], in_=ob[0:64, 0:w]),
                    reads=[R("ob")], writes=[(o_region, h, j0)],
                    group=f"{o_group}_{h}_{j0}")
    return name


def build_L0A1(S, D, HPC, DK=64):
    nc = bass.Bass("TRN2", target_bir_lowering=False)
    xfull = nc.dram_tensor("xfull", [S, D], F32, kind="ExternalInput")
    wq = nc.dram_tensor("wq", [D, HPC * DK], F32, kind="ExternalInput")
    wk = nc.dram_tensor("wk", [D, HPC * DK], F32, kind="ExternalInput")
    wv = nc.dram_tensor("wv", [D, HPC * DK], F32, kind="ExternalInput")
    O = nc.dram_tensor("O", [HPC * DK, S], BF16, kind="ExternalOutput")
    xscr = nc.dram_tensor("xscr", [S, D], BF16)

    from contextlib import ExitStack
    with ExitStack() as stack:
        em = Emitter(nc)
        ctx = Ctx(nc, em, stack, n_f32_banks=7, transpose_bank=False)
        nD = D // 128
        x_fm = [ctx.sb(f"x_fm{c}", [128, S], BF16) for c in range(nD)]
        wq_sb = [ctx.sb(f"wq{c}", [128, HPC * DK], BF16) for c in range(nD)]
        wk_sb = [ctx.sb(f"wk{c}", [128, HPC * DK], BF16) for c in range(nD)]
        wv_sb = [ctx.sb(f"wv{c}", [128, HPC * DK], BF16) for c in range(nD)]

        for c in range(nD):
            for nm, dram, sb in (("wq", wq, wq_sb), ("wk", wk, wk_sb),
                                 ("wv", wv, wv_sb)):
                em.dma("gpsimd", lambda g, dram=dram, sb=sb, c=c: g.dma_start(
                    out=sb[c][:, :], in_=dram[c * 128:(c + 1) * 128, :]),
                    reads=[nm + "_hbm"], writes=[(nm, c)], group="w")

        load_fm(ctx, xfull, "x_hbm", S, D, xscr, x_fm, "xfm", "x")
        ln_fm(ctx, x_fm, "xfm", S, D, x_fm, "xln", banks=(0, 1))
        attn_inner(ctx, S, HPC, DK,
                   x_fm, "xln", D, x_fm, "xln", D,
                   wq_sb, "wq", wk_sb, "wk", wv_sb, "wv",
                   O, "O", "Ost", scale=1.0 / np.sqrt(DK))
        em.final_gate("sync", [k for k in em.last_write if isinstance(k, tuple)
                               and k[0] == "O"])
        with nc.Block() as block:
            em.finalize(block)
    return nc


def ln_fm2(ctx, x_chunks, x_region, T, D, out_chunks, out_region, x_f32=False,
           eps=1e-5, gamma_beta=None, tag="ln", banks=(0, 1)):
    return ln_fm(ctx, x_chunks, x_region, T, D, out_chunks, out_region,
                 eps=eps, gamma_beta=gamma_beta, tag=tag, banks=banks,
                 x_f32=x_f32)


def resid_add(ctx, r_chunks, r_region, add_chunks, add_region, T, nD):
    em = ctx.em
    for c in range(nD):
        for j0 in range(0, T, NMAX):
            j1 = min(j0 + NMAX, T)
            em.op("vector", lambda v, c=c, j0=j0, j1=j1: v.tensor_add(
                r_chunks[c][:, j0:j1], r_chunks[c][:, j0:j1], add_chunks[c][:, j0:j1]),
                reads=[(r_region, c), (add_region, c)], writes=[(r_region, c)])


def binfc(ctx, x_chunks, x_region, T, Din, FF, wT_dram, wT_region, bcol_sb, bcol_region,
          out_cb, gelu_act, tag, alpha_bank, h_banks=(2, 3)):
    em = ctx.em
    name = ctx.name(tag)
    R = lambda s: name + s
    nD = Din // 128
    BLK = min(FF, 1024)
    nblk = FF // BLK
    nfc = BLK // 128
    wb = [ctx.sb(name + f"_wb{c}", [128, BLK], BF16) for c in range(nD)]
    ab = ctx.sb(name + "_ab", [128, BLK], BF16)
    alpha = ctx.sb(name + "_al", [128, nfc], F32)
    ones = ctx.ones_col(128)
    abank = ctx.bank[alpha_bank]

    for blk in range(nblk):
        f0 = blk * BLK
        for c in range(nD):
            em.dma("gpsimd", lambda g, c=c, f0=f0: g.dma_start(
                out=wb[c][:, :], in_=wT_dram[c * 128:(c + 1) * 128, f0:f0 + BLK]),
                reads=[wT_region], writes=[(R("wb"), c)], group=R(f"w{blk}"))
        for c in range(nD):
            em.op("scalar", lambda s, c=c, blk=blk: s.activation(
                ab[:, :], wb[c][:, :], AF.Abs),
                reads=[(R("wb"), c)], writes=[R("ab")])
            for g in range(nfc):
                em.op("tensor", lambda t, c=c, g=g: t.matmul(
                    abank[:, g:g + 1], ab[:, g * 128:(g + 1) * 128], ones[:, :],
                    start=(c == 0), stop=(c == nD - 1)),
                    reads=[R("ab"), ("ones_col", 128)], writes=[pb(alpha_bank)],
                    accum=(c > 0))
        em.op("vector", lambda v: v.tensor_scalar_mul(
            alpha[:, 0:nfc], abank[:, 0:nfc], 1.0 / Din),
            reads=[pb(alpha_bank)], writes=[R("al")])
        for c in range(nD):
            em.op("scalar", lambda s, c=c, blk=blk: s.activation(
                wb[c][:, :], wb[c][:, :], AF.Sign),
                reads=[(R("wb"), c)], writes=[(R("wb"), c)])
        for g in range(nfc):
            bk = h_banks[g % len(h_banks)]
            for j0 in range(0, T, NMAX):
                j1 = min(j0 + NMAX, T)
                w = j1 - j0
                for c in range(nD):
                    em.op("tensor", lambda t, c=c, g=g, j0=j0, w=w, bk=bk: t.matmul(
                        ctx.bank[bk][:, 0:w], wb[c][:, g * 128:(g + 1) * 128],
                        x_chunks[c][:, j0:j0 + w],
                        start=(c == 0), stop=(c == nD - 1)),
                        reads=[(x_region, c), (R("wb"), c)], writes=[pb(bk)],
                        accum=(c > 0))
                out_cb(blk, g, j0, w, ctx.bank[bk][:, 0:w], pb(bk),
                       alpha[:, g:g + 1], R("al"))
    return name


def build_L0A2B(SH, D, FF):
    nc = bass.Bass("TRN2", target_bir_lowering=False)
    xT = nc.dram_tensor("xT", [D, SH], F32, kind="ExternalInput")
    Oo = nc.dram_tensor("Oo", [D, SH], BF16, kind="ExternalInput")
    woT = nc.dram_tensor("woT", [D, D], F32, kind="ExternalInput")
    w1T = nc.dram_tensor("w1T", [D, FF], F32, kind="ExternalInput")
    b1c = nc.dram_tensor("b1c", [128, FF // 128], F32, kind="ExternalInput")
    w2T = nc.dram_tensor("w2T", [FF, D], F32, kind="ExternalInput")
    b2c = nc.dram_tensor("b2c", [128, D // 128], F32, kind="ExternalInput")
    R1 = nc.dram_tensor("R1", [D, SH], F32, kind="ExternalOutput")
    rmid = nc.dram_tensor("rmid", [D, SH], F32)

    from contextlib import ExitStack
    with ExitStack() as stack:
        em = Emitter(nc)
        ctx = Ctx(nc, em, stack, n_f32_banks=7, transpose_bank=False)
        nD = D // 128
        rstrm = ctx.sb("rstrm", [128, SH // 2], F32)
        o_fm = [ctx.sb(f"o{c}", [128, SH], BF16) for c in range(nD)]
        wo_sb = [ctx.sb(f"wo{c}", [128, D], BF16) for c in range(nD)]
        ao_fm = [ctx.sb(f"ao{c}", [128, SH], BF16) for c in range(nD)]
        xln = [ctx.sb(f"xln{c}", [128, SH], BF16) for c in range(nD)]
        b1_sb = ctx.sb("b1s", [128, FF // 128], F32)
        b2_sb = ctx.sb("b2s", [128, D // 128], F32)
        A2B_BLK = min(FF, 512)
        acc2 = [ctx.sb(f"acc2_{c}", [128, SH], F32) for c in range(nD)]

        for c in range(nD):
            em.dma("sync", lambda s, c=c: s.dma_start(
                out=o_fm[c][:, :], in_=Oo[c * 128:(c + 1) * 128, :]),
                reads=["Oo"], writes=[("O", c)], group="ld")
            em.dma("gpsimd", lambda g, c=c: g.dma_start(
                out=wo_sb[c][:, :], in_=woT[c * 128:(c + 1) * 128, :]),
                reads=["woT"], writes=[("wo", c)], group="ldw")
        em.dma("gpsimd", lambda g: g.dma_start(out=b1_sb[:, :], in_=b1c[:, :]),
               reads=["b1c"], writes=["b1"], group="ldb")
        em.dma("gpsimd", lambda g: g.dma_start(out=b2_sb[:, :], in_=b2c[:, :]),
               reads=["b2c"], writes=["b2"], group="ldb")

        for e in range(nD):
            for j0 in range(0, SH, NMAX):
                j1 = min(j0 + NMAX, SH)
                w = j1 - j0
                for c in range(nD):
                    em.op("tensor", lambda t, e=e, c=c, j0=j0, w=w: t.matmul(
                        ctx.bank[2][:, 0:w], wo_sb[c][:, e * 128:(e + 1) * 128],
                        o_fm[c][:, j0:j0 + w],
                        start=(c == 0), stop=(c == nD - 1)),
                        reads=[("wo", c), ("O", c)], writes=[pb(2)], accum=(c > 0))
                em.op("vector", lambda v, e=e, j0=j0, w=w: v.tensor_copy(
                    ao_fm[e][:, j0:j0 + w], ctx.bank[2][:, 0:w]),
                    reads=[pb(2)], writes=[("ao", e)])

        ln_fm(ctx, ao_fm, "ao", SH, D, ao_fm, "aoln", banks=(0, 1))
        HW2 = SH // 2
        for c in range(nD):
            for hh in range(2):
                em.dma("sync", lambda s, c=c, hh=hh: s.dma_start(
                    out=rstrm[:, :], in_=xT[c * 128:(c + 1) * 128, hh * HW2:(hh + 1) * HW2]),
                    reads=["xT"], writes=["rstrm"], group=f"rl{c}_{hh}")
                em.op("vector", lambda v, c=c, hh=hh: v.tensor_add(
                    rstrm[:, :], rstrm[:, :], ao_fm[c][:, hh * HW2:(hh + 1) * HW2]),
                    reads=["rstrm", ("aoln", c)], writes=["rstrm"])
                em.dma("sync", lambda s, c=c, hh=hh: s.dma_start(
                    out=rmid[c * 128:(c + 1) * 128, hh * HW2:(hh + 1) * HW2], in_=rstrm[:, :]),
                    reads=["rstrm"], writes=[("rmid", c)], group=f"rs{c}_{hh}")
        for c in range(nD):
            em.dma("gpsimd", lambda g, c=c: g.dma_start(
                out=xln[c][:, :], in_=rmid[c * 128:(c + 1) * 128, :]),
                reads=[("rmid", c)], writes=[("xln", c)], group="rmb")
        ln_fm(ctx, xln, "xln", SH, D, xln, "xln", banks=(0, 1))

        for c in range(nD):
            em.op("gpsimd", lambda g, c=c: g.memset(acc2[c][:, :], 0.0),
                  writes=[("acc2", c)])

        BLK = A2B_BLK
        nblk = FF // BLK
        name2 = ctx.name("fc2")
        nfc = BLK // 128
        if SH >= BLK * 2:
            h_fm = [o_fm[c] for c in range(nfc)]
            w1b = [o_fm[4 + c // 4][:, (c % 4) * BLK:(c % 4 + 1) * BLK]
                   for c in range(nD)]
            w2b = [o_fm[6 + c // 2][:, (c % 2) * D:(c % 2 + 1) * D]
                   for c in range(nfc)]
            for c in range(nfc):
                em.adopt(("h", c), [("O", cc) for cc in range(nD)])
                em.adopt(("w2b", c), [("O", cc) for cc in range(nD)])
            for c in range(nD):
                em.adopt(("w1b", c), [("O", cc) for cc in range(nD)])
        else:
            h_fm = [ctx.sb(f"h{c}", [128, SH], BF16) for c in range(nfc)]
            w1b = [ctx.sb("fc1_wb%d" % c, [128, BLK], BF16) for c in range(nD)]
            w2b = [ctx.sb(name2 + f"_wb{c}", [128, D], BF16) for c in range(nfc)]
        ab2 = ctx.sb(name2 + "_ab", [128, D], BF16)
        alpha2ps = 6
        ones = ctx.ones_col(128)
        ab1 = ctx.sb("fc1_ab", [128, BLK], BF16)
        alpha1 = ctx.sb("fc1_al", [128, nfc], F32)

        for blk in range(nblk):
            f0 = blk * BLK
            for c in range(nD):
                em.dma("gpsimd", lambda g, c=c, f0=f0: g.dma_start(
                    out=w1b[c][:, :], in_=w1T[c * 128:(c + 1) * 128, f0:f0 + BLK]),
                    reads=["w1T"], writes=[("w1b", c)], group=f"w1_{blk}")
            for c in range(nD):
                em.op("scalar", lambda s, c=c, blk=blk: s.activation(
                    ab1[:, :], w1b[c][:, :], AF.Abs),
                    reads=[("w1b", c)], writes=["ab1"])
                for g in range(nfc):
                    em.op("tensor", lambda t, c=c, g=g: t.matmul(
                        ctx.bank[4][:, g:g + 1], ab1[:, g * 128:(g + 1) * 128],
                        ones[:, :], start=(c == 0 and g == 0),
                        stop=(c == nD - 1 and g == nfc - 1)),
                        reads=["ab1", ("ones_col", 128)], writes=[pb(4)],
                        accum=not (c == 0 and g == 0))
            em.op("vector", lambda v: v.tensor_scalar_mul(
                alpha1[:, 0:nfc], ctx.bank[4][:, 0:nfc], 1.0 / D),
                reads=[pb(4)], writes=["al1"])
            for c in range(nD):
                em.op("scalar", lambda s, c=c, blk=blk: s.activation(
                    w1b[c][:, :], w1b[c][:, :], AF.Sign),
                    reads=[("w1b", c)], writes=[("w1b", c)])
            for g in range(nfc):
                bk = 2 + (g % 2)
                for j0 in range(0, SH, NMAX):
                    j1 = min(j0 + NMAX, SH)
                    w = j1 - j0
                    for c in range(nD):
                        em.op("tensor", lambda t, c=c, g=g, j0=j0, w=w, bk=bk, blk=blk: t.matmul(
                            ctx.bank[bk][:, 0:w], w1b[c][:, g * 128:(g + 1) * 128],
                            xln[c][:, j0:j0 + w],
                            start=(c == 0), stop=(c == nD - 1)),
                            reads=[("xln", c), ("w1b", c)], writes=[pb(bk)],
                            accum=(c > 0))
                    em.op("scalar", lambda s, g=g, j0=j0, w=w, bk=bk, blk=blk: s.activation(
                        h_fm[g][:, j0:j0 + w], ctx.bank[bk][:, 0:w], AF.Gelu,
                        bias=b1_sb[:, blk * nfc + g: blk * nfc + g + 1],
                        scale=alpha1[:, g:g + 1]),
                        reads=[pb(bk), "al1", "b1"], writes=[("h", g)])

            for c in range(nfc):
                em.dma("gpsimd", lambda g, c=c, f0=f0: g.dma_start(
                    out=w2b[c][:, :], in_=w2T[f0 + c * 128:f0 + (c + 1) * 128, :]),
                    reads=["w2T"], writes=[("w2b", c)], group=f"w2_{blk}")
            for c in range(nfc):
                em.op("scalar", lambda s, c=c, blk=blk: s.activation(
                    ab2[:, :], w2b[c][:, :], AF.Abs),
                    reads=[("w2b", c)], writes=["ab2"])
                for g in range(nD):
                    gi = blk * nfc + c
                    first = (gi == 0 and g == 0)
                    last = (gi == FF // 128 - 1 and g == nD - 1)
                    em.op("tensor", lambda t, c=c, g=g, first=first, last=last: t.matmul(
                        ctx.bank[alpha2ps][:, g:g + 1], ab2[:, g * 128:(g + 1) * 128],
                        ones[:, :], start=first, stop=last),
                        reads=["ab2", ("ones_col", 128)], writes=[pb(alpha2ps)],
                        accum=not first)
            for c in range(nfc):
                em.op("scalar", lambda s, c=c, blk=blk: s.activation(
                    w2b[c][:, :], w2b[c][:, :], AF.Sign),
                    reads=[("w2b", c)], writes=[("w2b", c)])
            for e in range(nD):
                bk = 2 + (e % 2)
                for j0 in range(0, SH, NMAX):
                    j1 = min(j0 + NMAX, SH)
                    w = j1 - j0
                    for c in range(nfc):
                        em.op("tensor", lambda t, c=c, e=e, j0=j0, w=w, bk=bk, blk=blk: t.matmul(
                            ctx.bank[bk][:, 0:w], w2b[c][:, e * 128:(e + 1) * 128],
                            h_fm[c][:, j0:j0 + w],
                            start=(c == 0), stop=(c == nfc - 1)),
                            reads=[("h", c), ("w2b", c)], writes=[pb(bk)],
                            accum=(c > 0))
                    em.op("vector", lambda v, e=e, j0=j0, w=w, bk=bk: v.tensor_add(
                        acc2[e][:, j0:j0 + w], acc2[e][:, j0:j0 + w],
                        ctx.bank[bk][:, 0:w]),
                        reads=[pb(bk), ("acc2", e)], writes=[("acc2", e)])

        alpha2 = ctx.sb("al2", [128, nD], F32)
        em.op("vector", lambda v: v.tensor_scalar_mul(
            alpha2[:, 0:nD], ctx.bank[alpha2ps][:, 0:nD], 1.0 / FF),
            reads=[pb(alpha2ps)], writes=["al2"])
        out2 = ao_fm
        for e in range(nD):
            for j0 in range(0, SH, NMAX):
                j1 = min(j0 + NMAX, SH)
                em.op("vector", lambda v, e=e, j0=j0, j1=j1: v.tensor_scalar(
                    out2[e][:, j0:j1], acc2[e][:, j0:j1], alpha2[:, e:e + 1],
                    b2_sb[:, e:e + 1], ALU.mult, ALU.add),
                    reads=[("acc2", e), "al2", "b2"], writes=[("out2", e)])
        ln_fm(ctx, out2, "out2", SH, D, out2, "out2ln", banks=(0, 1))
        for c in range(nD):
            for hh in range(2):
                em.dma("sync", lambda s, c=c, hh=hh: s.dma_start(
                    out=rstrm[:, :], in_=rmid[c * 128:(c + 1) * 128, hh * HW2:(hh + 1) * HW2]),
                    reads=[("rmid", c)], writes=["rstrm"], group=f"r2l{c}_{hh}")
                em.op("vector", lambda v, c=c, hh=hh: v.tensor_add(
                    rstrm[:, :], rstrm[:, :], out2[c][:, hh * HW2:(hh + 1) * HW2]),
                    reads=["rstrm", ("out2ln", c)], writes=["rstrm"])
                em.dma("sync", lambda s, c=c, hh=hh: s.dma_start(
                    out=R1[c * 128:(c + 1) * 128, hh * HW2:(hh + 1) * HW2], in_=rstrm[:, :]),
                    reads=["rstrm"], writes=[("R1", c)], group=f"r2s{c}_{hh}")
        em.final_gate("sync", [("R1", c) for c in range(nD)])
        with nc.Block() as block:
            em.finalize(block)
    return nc


def build_L1D1(S, D, L, HPC, DK=64):
    nc = bass.Bass("TRN2", target_bir_lowering=False)
    xfull = nc.dram_tensor("xfull", [S, D], F32, kind="ExternalInput")
    wdkv = nc.dram_tensor("wdkv", [D, L], F32, kind="ExternalInput")
    bdkvc = nc.dram_tensor("bdkvc", [128, L // 128], F32, kind="ExternalInput")
    wdq = nc.dram_tensor("wdq", [D, L], F32, kind="ExternalInput")
    bdqc = nc.dram_tensor("bdqc", [128, L // 128], F32, kind="ExternalInput")
    wuq = nc.dram_tensor("wuq", [L, HPC * DK], F32, kind="ExternalInput")
    buqr = nc.dram_tensor("buqr", [1, HPC * DK], F32, kind="ExternalInput")
    wuk = nc.dram_tensor("wuk", [L, HPC * DK], F32, kind="ExternalInput")
    bukr = nc.dram_tensor("bukr", [1, HPC * DK], F32, kind="ExternalInput")
    wuv = nc.dram_tensor("wuv", [L, HPC * DK], F32, kind="ExternalInput")
    buvr = nc.dram_tensor("buvr", [1, HPC * DK], F32, kind="ExternalInput")
    O = nc.dram_tensor("O", [HPC * DK, S], BF16, kind="ExternalOutput")
    xscr = nc.dram_tensor("xscr", [S, D], BF16)

    from contextlib import ExitStack
    with ExitStack() as stack:
        em = Emitter(nc)
        ctx = Ctx(nc, em, stack, n_f32_banks=7, transpose_bank=False)
        nD = D // 128
        nL = L // 128
        x_fm = [ctx.sb(f"x_fm{c}", [128, NMAX], BF16) for c in range(nD)]
        ckv = [ctx.sb(f"ckv{c}", [128, S], BF16) for c in range(nL)]
        cq = [ctx.sb(f"cq{c}", [128, S], BF16) for c in range(nL)]
        wdkv_sb = [ctx.sb(f"wdkv{c}", [128, L], BF16) for c in range(nD)]
        wdq_sb = [ctx.sb(f"wdq{c}", [128, L], BF16) for c in range(nD)]
        wuq_sb = [ctx.sb(f"wuq{c}", [128, HPC * DK], BF16) for c in range(nL)]
        wuk_sb = [ctx.sb(f"wuk{c}", [128, HPC * DK], BF16) for c in range(nL)]
        wuv_sb = [ctx.sb(f"wuv{c}", [128, HPC * DK], BF16) for c in range(nL)]
        bdkv_sb = ctx.sb("bdkv", [128, L // 128], F32)
        bdq_sb = ctx.sb("bdq", [128, L // 128], F32)
        bu_sb = {}
        for nm in ("buq", "buk", "buv"):
            bu_sb[nm] = ctx.sb(nm + "b", [128, HPC * DK], F32)
        burow = {nm: ctx.sb(nm + "rw", [1, HPC * DK], F32)
                 for nm in ("buq", "buk", "buv")}

        for c in range(nD):
            for nm, dram, sb in (("wdkv", wdkv, wdkv_sb), ("wdq", wdq, wdq_sb)):
                em.dma("gpsimd", lambda g, dram=dram, sb=sb, c=c: g.dma_start(
                    out=sb[c][:, :], in_=dram[c * 128:(c + 1) * 128, :]),
                    reads=[nm + "_hbm"], writes=[(nm, c)], group="w")
        for c in range(nL):
            for nm, dram, sb in (("wuq", wuq, wuq_sb), ("wuk", wuk, wuk_sb),
                                 ("wuv", wuv, wuv_sb)):
                em.dma("gpsimd", lambda g, dram=dram, sb=sb, c=c: g.dma_start(
                    out=sb[c][:, :], in_=dram[c * 128:(c + 1) * 128, :]),
                    reads=[nm + "_hbm"], writes=[(nm, c)], group="w")
        em.dma("gpsimd", lambda g: g.dma_start(out=bdkv_sb[:, :], in_=bdkvc[:, :]),
               reads=["bdkv_hbm"], writes=["bdkv"], group="w2")
        em.dma("gpsimd", lambda g: g.dma_start(out=bdq_sb[:, :], in_=bdqc[:, :]),
               reads=["bdq_hbm"], writes=["bdq"], group="w2")
        for nm, dram in (("buq", buqr), ("buk", bukr), ("buv", buvr)):
            em.dma("gpsimd", lambda g, dram=dram, nm=nm: g.dma_start(
                out=burow[nm][:, :], in_=dram[:, :]),
                reads=[nm + "_hbm"], writes=[nm + "row"], group="w2")
        bcast_row(ctx, "buvrow", burow["buv"], bu_sb["buv"], "buvbc", 128,
                  HPC * DK, 5)

        for r0 in range(0, S, 512):
            r1c = min(r0 + 512, S)
            em.dma("gpsimd", lambda g, r0=r0, r1c=r1c: g.dma_start(
                out=xscr[r0:r1c, :], in_=xfull[r0:r1c, :]),
                reads=["x_hbm"], writes=[("xscr", r0)], group="xc")
        for j0 in range(0, S, NMAX):
            j1 = min(j0 + NMAX, S)
            w = j1 - j0
            for c in range(nD):
                em.adopt(("xfm", c), [("xln", c)])
                em.dma("sync", lambda sy, c=c, j0=j0, j1=j1: sy.dma_start_transpose(
                    x_fm[c][:, 0:j1 - j0], xscr[j0:j1, c * 128:(c + 1) * 128]),
                    reads=[("xscr", j0)], writes=[("xfm", c)], group=f"xt{j0}")
            ln_fm(ctx, x_fm, "xfm", w, D, x_fm, "xln", banks=(0, 1), tag=f"ln{j0}")
            for dst, dreg, w_sb, wreg, b_sb, breg in (
                    (ckv, "ckv", wdkv_sb, "wdkv", bdkv_sb, "bdkv"),
                    (cq, "cq", wdq_sb, "wdq", bdq_sb, "bdq")):
                for lc in range(nL):
                    for c in range(nD):
                        em.op("tensor", lambda t, lc=lc, c=c, w=w, w_sb=w_sb: t.matmul(
                            ctx.bank[2][:, 0:w], w_sb[c][:, lc * 128:(lc + 1) * 128],
                            x_fm[c][:, 0:w],
                            start=(c == 0), stop=(c == nD - 1)),
                            reads=[("xln", c), (wreg, c)], writes=[pb(2)],
                            accum=(c > 0))
                    em.op("vector", lambda v, lc=lc, j0=j0, w=w, dst=dst, b_sb=b_sb: v.tensor_scalar_add(
                        dst[lc][:, j0:j0 + w], ctx.bank[2][:, 0:w],
                        b_sb[:, lc:lc + 1]),
                        reads=[pb(2), breg], writes=[(dreg, lc)])

        attn_inner(ctx, S, HPC, DK,
                   cq, "cq", L, ckv, "ckv", L,
                   wuq_sb, "wuq", wuk_sb, "wuk", wuv_sb, "wuv",
                   O, "O", "Ost", scale=1.0 / np.sqrt(DK),
                   qkv_bias=(burow["buq"], "buqrow", burow["buk"], "bukrow",
                             bu_sb["buv"], "buvbc"))
        em.final_gate("sync", [k for k in em.last_write if isinstance(k, tuple)
                               and k[0] == "O"])
        with nc.Block() as block:
            em.finalize(block)
    return nc


def build_L1D2(SH, D, FF, E=8):
    nc = bass.Bass("TRN2", target_bir_lowering=False)
    xT = nc.dram_tensor("xT", [D, SH], F32, kind="ExternalInput")
    Oo = nc.dram_tensor("Oo", [D, SH], BF16, kind="ExternalInput")
    wo = nc.dram_tensor("wo", [D, D], F32, kind="ExternalInput")
    boc = nc.dram_tensor("boc", [128, D // 128], F32, kind="ExternalInput")
    g1c = nc.dram_tensor("g1c", [128, D // 128], F32, kind="ExternalInput")
    be1c = nc.dram_tensor("be1c", [128, D // 128], F32, kind="ExternalInput")
    g2c = nc.dram_tensor("g2c", [128, D // 128], F32, kind="ExternalInput")
    be2c = nc.dram_tensor("be2c", [128, D // 128], F32, kind="ExternalInput")
    wg = nc.dram_tensor("wg", [D, E], F32, kind="ExternalInput")
    bgr = nc.dram_tensor("bgr", [1, E], F32, kind="ExternalInput")
    wn = nc.dram_tensor("wn", [D, E], F32, kind="ExternalInput")
    bnr = nc.dram_tensor("bnr", [1, E], F32, kind="ExternalInput")
    noise = nc.dram_tensor("noise", [SH, E], F32, kind="ExternalInput")
    W1 = nc.dram_tensor("W1", [E * D, FF], F32, kind="ExternalInput")
    b1cs = nc.dram_tensor("b1cs", [128, E * FF // 128], F32, kind="ExternalInput")
    W2 = nc.dram_tensor("W2", [E * FF, D], F32, kind="ExternalInput")
    b2cs = nc.dram_tensor("b2cs", [128, E * D // 128], F32, kind="ExternalInput")
    Rout = nc.dram_tensor("Rout", [D, SH], F32, kind="ExternalOutput")
    wtm_dram = nc.dram_tensor("wtm_scr", [SH, 16], BF16)

    from contextlib import ExitStack
    with ExitStack() as stack:
        em = Emitter(nc)
        ctx = Ctx(nc, em, stack, n_f32_banks=6, transpose_bank=True)
        nD = D // 128
        BLK = min(FF, 512)
        nblk = FF // BLK
        nfc = BLK // 128
        ntt = SH // 128
        o_fm = [ctx.sb(f"o{c}", [128, SH], BF16) for c in range(nD)]
        wo_sb = [ctx.sb(f"wo{c}", [128, 128], BF16) for c in range(nD)]
        xb = [ctx.sb(f"xb{c}", [128, SH], BF16) for c in range(nD)]
        rc2 = ctx.sb("rc2", [128, SH // 2], F32)
        x1 = [ctx.sb(f"x1_{c}", [128, SH], BF16) for c in range(nD)]
        acc = [ctx.sb(f"acc{c}", [128, SH], F32) for c in range(nD)]
        if SH >= BLK and nD >= 8:
            h_fm = [xb[c] for c in range(nfc)]
            for c in range(nfc):
                em.adopt(("h", c), [("xb", cc) for cc in range(nD)])
            w1b = [o_fm[4 + c // 4][:, (c % 4) * BLK:(c % 4 + 1) * BLK]
                   for c in range(nD)]
            for c in range(nD):
                em.adopt(("w1b", c), [("O", cc) for cc in range(nD)])
        else:
            h_fm = [ctx.sb(f"h{c}", [128, SH], BF16) for c in range(nfc)]
            w1b = [ctx.sb(f"w1b{c}", [128, BLK], BF16) for c in range(nD)]
        if SH >= 2 * D:
            w2b = [o_fm[c // 2][:, (c % 2) * D:(c % 2 + 1) * D] for c in range(nfc)]
            for c in range(nfc):
                em.adopt(("w2b", c), [("O", cc) for cc in range(nD)])
        else:
            w2b = [ctx.sb(f"w2b{c}", [128, D], BF16) for c in range(nfc)]
        tmpm = ctx.sb("tmpm", [128, NMAX], BF16)
        cols = {}
        for nm, dram in (("bo", boc), ("g1", g1c), ("be1", be1c), ("g2", g2c),
                         ("be2", be2c)):
            cols[nm] = ctx.sb(nm + "cs", [128, nD], F32)
            em.dma("gpsimd", lambda g, dram=dram, nm=nm: g.dma_start(
                out=cols[nm][:, :], in_=dram[:, :]),
                reads=[nm + "_hbm"], writes=[nm], group="small")
        wg_sb = [ctx.sb(f"wg{c}", [128, E], BF16) for c in range(nD)]
        wn_sb = [ctx.sb(f"wn{c}", [128, E], BF16) for c in range(nD)]
        for c in range(nD):
            em.dma("gpsimd", lambda g, c=c: g.dma_start(
                out=wg_sb[c][:, :], in_=wg[c * 128:(c + 1) * 128, :]),
                reads=["wg_hbm"], writes=[("wg", c)], group="small")
            em.dma("gpsimd", lambda g, c=c: g.dma_start(
                out=wn_sb[c][:, :], in_=wn[c * 128:(c + 1) * 128, :]),
                reads=["wn_hbm"], writes=[("wn", c)], group="small")
        bg_row = ctx.sb("bgrw", [1, E], F32)
        bn_row = ctx.sb("bnrw", [1, E], F32)
        bg_rep = ctx.sb("bgrep", [128, E], F32)
        bn_rep = ctx.sb("bnrep", [128, E], F32)
        em.dma("gpsimd", lambda g: g.dma_start(out=bg_row[:, :], in_=bgr[:, :]),
               reads=["bgr_hbm"], writes=["bgrow"], group="small")
        em.dma("gpsimd", lambda g: g.dma_start(out=bn_row[:, :], in_=bnr[:, :]),
               reads=["bnr_hbm"], writes=["bnrow"], group="small")
        bcast_row(ctx, "bgrow", bg_row, bg_rep, "bgrep", 128, E, 5)
        bcast_row(ctx, "bnrow", bn_row, bn_rep, "bnrep", 128, E, 5)
        noi = ctx.sb("noi", [128, ntt * E], F32)
        b1cs_sb = ctx.sb("b1css", [128, E * FF // 128], F32)
        b2cs_sb = ctx.sb("b2css", [128, E * nD], F32)
        em.dma("gpsimd", lambda g: g.dma_start(out=b1cs_sb[:, :], in_=b1cs[:, :]),
               reads=["b1cs_hbm"], writes=["b1cs"], group="small2")
        em.dma("gpsimd", lambda g: g.dma_start(out=b2cs_sb[:, :], in_=b2cs[:, :]),
               reads=["b2cs_hbm"], writes=["b2cs"], group="small2")
        for tt in range(ntt):
            em.dma("sync", lambda s, tt=tt: s.dma_start(
                out=noi[:, tt * E:(tt + 1) * E], in_=noise[tt * 128:(tt + 1) * 128, :]),
                reads=["noise_hbm"], writes=[("noi", tt)], group="noi")

        for c in range(nD):
            em.dma("gpsimd", lambda g, c=c: g.dma_start(
                out=xb[c][:, :], in_=xT[c * 128:(c + 1) * 128, :]),
                reads=["xT_hbm"], writes=[("xb", c)], group="ld")
            em.dma("sync", lambda s, c=c: s.dma_start(
                out=o_fm[c][:, :], in_=Oo[c * 128:(c + 1) * 128, :]),
                reads=["Oo_hbm"], writes=[("O", c)], group="ld")
        ln_fm(ctx, xb, "xb", SH, D, xb, "xb", banks=(0, 1))

        for e in range(nD):
            for c in range(nD):
                em.dma("gpsimd", lambda g, c=c, e=e: g.dma_start(
                    out=wo_sb[c][:, :], in_=wo[c * 128:(c + 1) * 128,
                                              e * 128:(e + 1) * 128]),
                    reads=["wo_hbm"], writes=[("wo", c)], group=f"ldw{e}")
            for j0 in range(0, SH, NMAX):
                j1 = min(j0 + NMAX, SH)
                w = j1 - j0
                for c in range(nD):
                    em.op("tensor", lambda t, e=e, c=c, j0=j0, w=w: t.matmul(
                        ctx.bank[2][:, 0:w], wo_sb[c][:, :],
                        o_fm[c][:, j0:j0 + w],
                        start=(c == 0), stop=(c == nD - 1)),
                        reads=[("wo", c), ("O", c)], writes=[pb(2)], accum=(c > 0))
                em.op("vector", lambda v, e=e, j0=j0, w=w: v.tensor_scalar_add(
                    tmpm[:, 0:w], ctx.bank[2][:, 0:w], cols["bo"][:, e:e + 1]),
                    reads=[pb(2), "bo"], writes=["tmpm"])
                em.op("vector", lambda v, e=e, j0=j0, w=w: v.tensor_add(
                    x1[e][:, j0:j0 + w], tmpm[:, 0:w], xb[e][:, j0:j0 + w]),
                    reads=["tmpm", ("xb", e)], writes=[("x1p", e)])
        g1cols = [cols["g1"][:, c:c + 1] for c in range(nD)]
        b1cols_ln = [cols["be1"][:, c:c + 1] for c in range(nD)]
        ln_fm(ctx, x1, "x1p", SH, D, x1, "x1", banks=(0, 1),
              gamma_beta=(g1cols, b1cols_ln, ["g1", "be1"]))

        Wtm = ctx.sb("Wtm", [128, 16], BF16)
        wfm16 = ctx.sb("wfm16", [16, SH], BF16)
        gsb = ctx.sb("gsb", [128, E], F32)
        nsb = ctx.sb("nsb", [128, E], F32)
        t8a = ctx.sb("t8a", [128, E], F32)
        t8b = ctx.sb("t8b", [128, E], F32)
        m1 = ctx.sb("m1", [128, 1], F32)
        m2 = ctx.sb("m2", [128, 1], F32)
        p1 = ctx.sb("p1", [128, 1], F32)
        p2 = ctx.sb("p2", [128, 1], F32)
        ident = ctx.identity(128)
        em.op("gpsimd", lambda g: g.memset(Wtm[:, :], 0.0), writes=["Wtm_z"])
        for tt in range(ntt):
            for dst, wsb, wreg, brep, breg, dreg in (
                    (gsb, wg_sb, "wg", bg_rep, "bgrep", "gsb"),
                    (nsb, wn_sb, "wn", bn_rep, "bnrep", "nsb")):
                for c in range(nD):
                    em.op("tensor", lambda t, tt=tt, c=c, wsb=wsb: t.matmul(
                        ctx.bank[3][:, 0:E], x1[c][:, tt * 128:(tt + 1) * 128],
                        wsb[c][:, :], start=(c == 0), stop=(c == nD - 1)),
                        reads=[("x1", c), (wreg, c)], writes=[pb(3)], accum=(c > 0))
                em.op("vector", lambda v, dst=dst, brep=brep: v.tensor_add(
                    dst[:, :], ctx.bank[3][:, 0:E], brep[:, :]),
                    reads=[pb(3), breg], writes=[dreg])
            em.op("scalar", lambda s: s.activation(t8a[:, :], nsb[:, :], AF.Abs),
                  reads=["nsb"], writes=["t8a"])
            em.op("vector", lambda v: v.tensor_scalar_mul(t8a[:, :], t8a[:, :], -1.0),
                  reads=["t8a"], writes=["t8a"])
            em.op("scalar", lambda s: s.activation(t8a[:, :], t8a[:, :], AF.Exp),
                  reads=["t8a"], writes=["t8a"])
            em.op("vector", lambda v: v.tensor_single_scalar(t8a[:, :], t8a[:, :],
                                                             1.0, ALU.add),
                  reads=["t8a"], writes=["t8a"])
            em.op("scalar", lambda s: s.activation(t8a[:, :], t8a[:, :], AF.Ln),
                  reads=["t8a"], writes=["t8a"])
            em.op("vector", lambda v: v.tensor_scalar_max(t8b[:, :], nsb[:, :], 0.0),
                  reads=["nsb"], writes=["t8b"])
            em.op("vector", lambda v: v.tensor_add(t8a[:, :], t8a[:, :], t8b[:, :]),
                  reads=["t8a", "t8b"], writes=["t8a"])
            em.op("vector", lambda v, tt=tt: v.tensor_mul(
                t8a[:, :], t8a[:, :], noi[:, tt * E:(tt + 1) * E]),
                reads=["t8a", ("noi", tt)], writes=["t8a"])
            em.op("vector", lambda v: v.tensor_add(gsb[:, :], gsb[:, :], t8a[:, :]),
                  reads=["gsb", "t8a"], writes=["gsb"])
            em.op("vector", lambda v: v.reduce_max(m1[:, :], gsb[:, :], axis=AX),
                  reads=["gsb"], writes=["m1"])
            em.op("vector", lambda v: v.tensor_single_scalar(
                t8a[:, :], gsb[:, :], m1[:, :], ALU.is_ge),
                reads=["gsb", "m1"], writes=["t8a"])
            em.op("vector", lambda v: v.scalar_tensor_tensor(
                t8b[:, :], t8a[:, :], -1e30, gsb[:, :], ALU.mult, ALU.add),
                reads=["t8a", "gsb"], writes=["t8b"])
            em.op("vector", lambda v: v.reduce_max(m2[:, :], t8b[:, :], axis=AX),
                  reads=["t8b"], writes=["m2"])
            em.op("vector", lambda v: v.tensor_single_scalar(
                t8b[:, :], t8b[:, :], m2[:, :], ALU.is_ge),
                reads=["t8b", "m2"], writes=["t8b"])
            em.op("vector", lambda v: v.tensor_sub(p1[:, :], m1[:, :], m2[:, :]),
                  reads=["m1", "m2"], writes=["p1"])
            em.op("scalar", lambda s: s.activation(p1[:, :], p1[:, :], AF.Sigmoid),
                  reads=["p1"], writes=["p1"])
            em.op("vector", lambda v: v.tensor_scalar(p2[:, :], p1[:, :], -1.0, 1.0,
                                                      ALU.mult, ALU.add),
                  reads=["p1"], writes=["p2"])
            em.op("vector", lambda v: v.tensor_scalar_mul(t8a[:, :], t8a[:, :],
                                                          p1[:, :]),
                  reads=["t8a", "p1"], writes=["t8a"])
            em.op("vector", lambda v: v.tensor_scalar_mul(t8b[:, :], t8b[:, :],
                                                          p2[:, :]),
                  reads=["t8b", "p2"], writes=["t8b"])
            em.op("vector", lambda v: v.tensor_add(Wtm[:, 0:E], t8a[:, :], t8b[:, :]),
                  reads=["t8a", "t8b", "Wtm_z"], writes=["Wtm"])
            em.op("tensor", lambda t, tt=tt: t.transpose(
                ctx.bankT[0:16, 0:128], Wtm[:, 0:16], ident[:, :]),
                reads=["Wtm", "Wtm_z", ("ident", 128)], writes=["pbT"])
            em.op("vector", lambda v, tt=tt: v.tensor_copy(
                wfm16[:, tt * 128:(tt + 1) * 128], ctx.bankT[0:16, 0:128]),
                reads=["pbT"], writes=["wfm16"])

        selsh2 = ctx.sb("selsh2", [16, 128], BF16)
        sels = [selsh2] * E

        def build_sel2(e):
            em.op("gpsimd", lambda g, e=e: g.iota(
                selsh2[:, :], [[0, 128]], base=-e, channel_multiplier=1,
                allow_small_or_imprecise_dtypes=True), writes=["selsh2"])
            em.op("vector", lambda v: v.tensor_single_scalar(
                selsh2[:, :], selsh2[:, :], 0.0, ALU.is_equal),
                reads=["selsh2"], writes=["selsh2"])

        for c in range(nD):
            em.op("gpsimd", lambda g, c=c: g.memset(acc[c][:, :], 0.0),
                  writes=[("acc", c)])

        web = ctx.sb("web", [128, SH], BF16)
        for e in range(E):
            build_sel2(e)
            for j0 in range(0, SH, NMAX):
                j1 = min(j0 + NMAX, SH)
                w = j1 - j0
                em.op("tensor", lambda t, e=e, j0=j0, w=w: t.matmul(
                    ctx.bank[5][:, 0:w], sels[e][:, :], wfm16[0:16, j0:j0 + w],
                    start=True, stop=True),
                    reads=["selsh2", "wfm16"], writes=[pb(5)])
                em.op("vector", lambda v, j0=j0, w=w: v.tensor_copy(
                    web[:, j0:j0 + w], ctx.bank[5][:, 0:w]),
                    reads=[pb(5)], writes=["web"])
            for blk in range(nblk):
                f0 = blk * BLK
                for c in range(nD):
                    em.dma("gpsimd", lambda g, c=c, f0=f0, e=e: g.dma_start(
                        out=w1b[c][:, :],
                        in_=W1[e * D + c * 128: e * D + (c + 1) * 128, f0:f0 + BLK]),
                        reads=["W1_hbm"], writes=[("w1b", c)],
                        group=f"w1_{e}_{blk}")
                for g_ in range(nfc):
                    bk = 2 + (g_ % 2)
                    gi = blk * nfc + g_
                    for j0 in range(0, SH, NMAX):
                        j1 = min(j0 + NMAX, SH)
                        w = j1 - j0
                        for c in range(nD):
                            em.op("tensor", lambda t, c=c, g_=g_, j0=j0, w=w, bk=bk, e=e, blk=blk: t.matmul(
                                ctx.bank[bk][:, 0:w], w1b[c][:, g_ * 128:(g_ + 1) * 128],
                                x1[c][:, j0:j0 + w],
                                start=(c == 0), stop=(c == nD - 1)),
                                reads=[("x1", c), ("w1b", c)], writes=[pb(bk)],
                                accum=(c > 0))
                        em.op("scalar", lambda s, g_=g_, j0=j0, w=w, bk=bk, e=e, gi=gi: s.activation(
                            h_fm[g_][:, j0:j0 + w], ctx.bank[bk][:, 0:w], AF.Relu,
                            bias=b1cs_sb[:, e * (FF // 128) + gi: e * (FF // 128) + gi + 1]),
                            reads=[pb(bk), "b1cs"], writes=[("h", g_)])
                for c in range(nfc):
                    em.dma("gpsimd", lambda g, c=c, f0=f0, e=e: g.dma_start(
                        out=w2b[c][:, :],
                        in_=W2[e * FF + f0 + c * 128: e * FF + f0 + (c + 1) * 128, :]),
                        reads=["W2_hbm"], writes=[("w2b", c)],
                        group=f"w2_{e}_{blk}")
                for d in range(nD):
                    bk = 2 + (d % 2)
                    for j0 in range(0, SH, NMAX):
                        j1 = min(j0 + NMAX, SH)
                        w = j1 - j0
                        for c in range(nfc):
                            em.op("tensor", lambda t, c=c, d=d, j0=j0, w=w, bk=bk, e=e, blk=blk: t.matmul(
                                ctx.bank[bk][:, 0:w], w2b[c][:, d * 128:(d + 1) * 128],
                                h_fm[c][:, j0:j0 + w],
                                start=(c == 0), stop=(c == nfc - 1)),
                                reads=[("h", c), ("w2b", c)], writes=[pb(bk)],
                                accum=(c > 0))
                        if blk == nblk - 1:
                            em.op("vector", lambda v, d=d, j0=j0, w=w, bk=bk, e=e: v.tensor_scalar_add(
                                tmpm[:, 0:w], ctx.bank[bk][:, 0:w],
                                b2cs_sb[:, e * nD + d: e * nD + d + 1]),
                                reads=[pb(bk), "b2cs"], writes=["tmpm"])
                            em.op("vector", lambda v, d=d, j0=j0, w=w: v.tensor_mul(
                                tmpm[:, 0:w], tmpm[:, 0:w], web[:, j0:j0 + w]),
                                reads=["tmpm", "web"], writes=["tmpm"])
                        else:
                            em.op("vector", lambda v, d=d, j0=j0, w=w, bk=bk: v.tensor_mul(
                                tmpm[:, 0:w], ctx.bank[bk][:, 0:w], web[:, j0:j0 + w]),
                                reads=[pb(bk), "web"], writes=["tmpm"])
                        em.op("vector", lambda v, d=d, j0=j0, w=w: v.tensor_add(
                            acc[d][:, j0:j0 + w], acc[d][:, j0:j0 + w], tmpm[:, 0:w]),
                            reads=[("acc", d), "tmpm"], writes=[("acc", d)])

        for c in range(nD):
            for j0 in range(0, SH, NMAX):
                j1 = min(j0 + NMAX, SH)
                em.op("vector", lambda v, c=c, j0=j0, j1=j1: v.tensor_add(
                    acc[c][:, j0:j1], acc[c][:, j0:j1], x1[c][:, j0:j1]),
                    reads=[("acc", c), ("x1", c)], writes=[("acc", c)])
        g2cols = [cols["g2"][:, c:c + 1] for c in range(nD)]
        b2cols_ln = [cols["be2"][:, c:c + 1] for c in range(nD)]
        bo2 = xb
        ln_fm(ctx, acc, "acc", SH, D, bo2, "bo2", banks=(0, 1), x_f32=True,
              gamma_beta=(g2cols, b2cols_ln, ["g2", "be2"]))
        ln_fm(ctx, bo2, "bo2", SH, D, bo2, "bo2ln", banks=(0, 1))
        HW2 = SH // 2
        for c in range(nD):
            for hh in range(2):
                em.dma("sync", lambda s, c=c, hh=hh: s.dma_start(
                    out=rc2[:, :], in_=xT[c * 128:(c + 1) * 128, hh * HW2:(hh + 1) * HW2]),
                    reads=["xT_hbm"], writes=["rc2"], group=f"r2l{c}_{hh}")
                em.op("vector", lambda v, c=c, hh=hh: v.tensor_add(
                    rc2[:, :], rc2[:, :], bo2[c][:, hh * HW2:(hh + 1) * HW2]),
                    reads=["rc2", ("bo2ln", c)], writes=["rc2"])
                em.dma("sync", lambda s, c=c, hh=hh: s.dma_start(
                    out=Rout[c * 128:(c + 1) * 128, hh * HW2:(hh + 1) * HW2], in_=rc2[:, :]),
                    reads=["rc2"], writes=[("Rout", c)], group=f"r2s{c}_{hh}")
        em.final_gate("sync", [("Rout", c) for c in range(nD)])
        with nc.Block() as block:
            em.finalize(block)
    return nc


def build_L1C1(S, D, H=4, DK=256, CH=64, SEG=512):
    DV = DK
    HD = H * DK
    nD = D // 128
    nHD = HD // 128
    nseg = S // SEG
    npc = SEG // CH
    nkc = DK // 128
    nc = bass.Bass("TRN2", target_bir_lowering=False)
    xfull = nc.dram_tensor("xfull", [S, D], F32, kind="ExternalInput")
    Wq = nc.dram_tensor("Wq", [D, HD], F32, kind="ExternalInput")
    Wk = nc.dram_tensor("Wk", [D, HD], F32, kind="ExternalInput")
    Wv = nc.dram_tensor("Wv", [D, HD], F32, kind="ExternalInput")
    Wb = nc.dram_tensor("Wb", [D, H], F32, kind="ExternalInput")
    cwq = nc.dram_tensor("cwq", [128, nHD * 4], F32, kind="ExternalInput")
    cwk = nc.dram_tensor("cwk", [128, nHD * 4], F32, kind="ExternalInput")
    cwv = nc.dram_tensor("cwv", [128, nHD * 4], F32, kind="ExternalInput")
    delta = nc.dram_tensor("delta", [HD, S], BF16, kind="ExternalOutput")
    vpad = nc.dram_tensor("vpad", [HD, S + 32], BF16, kind="ExternalOutput")
    xscr = nc.dram_tensor("xscr", [S, D], BF16)
    kscr = nc.dram_tensor("kscr", [HD, S], BF16)

    from contextlib import ExitStack
    with ExitStack() as stack:
        em = Emitter(nc)
        ctx = Ctx(nc, em, stack, n_f32_banks=7, transpose_bank=False)
        x_fm = [ctx.sb(f"x_fm{c}", [128, SEG], BF16) for c in range(nD)]
        wq_sb = [ctx.sb(f"wq{c}", [128, HD], BF16) for c in range(nD)]
        wk_sb = [ctx.sb(f"wk{c}", [128, HD], BF16) for c in range(nD)]
        wv_sb = [ctx.sb(f"wv{c}", [128, HD], BF16) for c in range(nD)]
        wb_sb = [ctx.sb(f"wb{c}", [128, H], BF16) for c in range(nD)]
        cw_sb = {}
        for nm, dram in (("cwq", cwq), ("cwk", cwk), ("cwv", cwv)):
            cw_sb[nm] = ctx.sb(nm + "s", [128, nHD * 4], F32)
            em.dma("gpsimd", lambda g, dram=dram, nm=nm: g.dma_start(
                out=cw_sb[nm][:, :], in_=dram[:, :]),
                reads=[nm + "_hbm"], writes=[nm], group="wsm")
        for c in range(nD):
            for nm, dram, sb in (("Wq", Wq, wq_sb), ("Wk", Wk, wk_sb),
                                 ("Wv", Wv, wv_sb), ("Wb", Wb, wb_sb)):
                em.dma("gpsimd", lambda g, dram=dram, sb=sb, c=c: g.dma_start(
                    out=sb[c][:, :], in_=dram[c * 128:(c + 1) * 128, :]),
                    reads=[nm + "_hbm"], writes=[(nm, c)], group="w")

        for r0 in range(0, S, 512):
            r1c = min(r0 + 512, S)
            em.dma("gpsimd", lambda g, r0=r0, r1c=r1c: g.dma_start(
                out=xscr[r0:r1c, :], in_=xfull[r0:r1c, :]),
                reads=["x_hbm"], writes=[("xscr", r0)], group="xc")
        beta_fm = ctx.sb("beta_fm", [H, S], BF16)
        beta_tm = ctx.sb("beta_tm", [128, (S // 128) * H], F32)
        selh = []
        for h in range(H):
            se = ctx.sb(f"selh{h}", [H, 128], BF16)
            em.op("gpsimd", lambda g, se=se, h=h: g.iota(
                se[:, :], [[0, 128]], base=-h, channel_multiplier=1,
                allow_small_or_imprecise_dtypes=True), writes=[("selh", h)])
            em.op("vector", lambda v, se=se: v.tensor_single_scalar(
                se[:, :], se[:, :], 0.0, ALU.is_equal),
                reads=[("selh", h)], writes=[("selh", h)])
            selh.append(se)
        bb = [ctx.sb(f"bb{h}", [128, SEG], BF16) for h in range(H)]

        ident = ctx.identity(128)
        maskA = ctx.sb("maskA", [128, 128], BF16)
        maskB = ctx.sb("maskB", [128, 128], BF16)
        maskL = ctx.sb("maskL", [128, 128], BF16)
        em.op("gpsimd", lambda g: g.iota(maskA[:, :], [[-1, 128]], channel_multiplier=1,
                                         allow_small_or_imprecise_dtypes=True),
              writes=["maskA"])
        em.op("vector", lambda v: v.tensor_single_scalar(maskA[:, :], maskA[:, :], 0.0,
                                                         ALU.is_gt),
              reads=["maskA"], writes=["maskA"])
        em.op("vector", lambda v: v.tensor_scalar_mul(maskA[:, :], maskA[:, :], -1.0),
              reads=["maskA"], writes=["maskA"])
        em.op("gpsimd", lambda g: g.iota(maskB[:, :], [[1, 128]], channel_multiplier=-1,
                                         allow_small_or_imprecise_dtypes=True),
              writes=["maskB"])
        em.op("vector", lambda v: v.tensor_single_scalar(maskB[:, :], maskB[:, :], 0.0,
                                                         ALU.is_gt),
              reads=["maskB"], writes=["maskB"])
        em.op("vector", lambda v: v.tensor_scalar_mul(maskB[:, :], maskB[:, :], -1.0),
              reads=["maskB"], writes=["maskB"])
        em.op("gpsimd", lambda g: g.iota(maskL[:, :], [[1, 128]], channel_multiplier=-1,
                                         allow_small_or_imprecise_dtypes=True),
              writes=["maskL"])
        em.op("vector", lambda v: v.tensor_single_scalar(maskL[:, :], maskL[:, :], 0.0,
                                                         ALU.is_ge),
              reads=["maskL"], writes=["maskL"])

        nseg_b = 1
        pre = [ctx.sb(f"pre{i}", [128, nHD * (SEG + 4)], BF16) for i in range(1)]
        qs = [[ctx.sb(f"qs{i}_{c}", [128, SEG], BF16) for c in range(nHD)]
              for i in range(nseg_b)]
        ks = [[ctx.sb(f"ks{i}_{c}", [128, SEG], BF16) for c in range(nHD)]
              for i in range(nseg_b)]
        vs = [[ctx.sb(f"vs{i}_{c}", [128, SEG], BF16) for c in range(nHD)]
              for i in range(nseg_b)]
        kbs = [[ctx.sb(f"kbs{i}_{c}", [128, SEG], BF16) for c in range(nHD)]
               for i in range(nseg_b)]
        kt = [ctx.sb(f"kt{i}", [128, (SEG // 128) * HD], BF16) for i in range(nseg_b)]
        vt = [ctx.sb(f"vt{i}", [128, (SEG // 128) * HD], BF16) for i in range(nseg_b)]
        ods = [ctx.sb(f"ods{i}", [128, nHD * SEG], BF16) for i in range(nseg_b)]
        tails = {nm: ctx.sb(f"tail{nm}", [128, nHD * 4], BF16) for nm in "qkv"}
        for nm in "qkv":
            em.op("gpsimd", lambda g, nm=nm: g.memset(tails[nm][:, :], 0.0),
                  writes=[("tail", nm)])
        nsq = ctx.sb("nsq", [128, SEG], BF16)
        nrow = ctx.sb("nrow", [1, SEG], F32)
        nbc = ctx.sb("nbc", [128, SEG], BF16)
        zpad = ctx.sb("zpad", [128, 32], BF16)
        em.op("gpsimd", lambda g: g.memset(zpad[:, :], 0.0), writes=["zpad"])
        for c in range(nHD):
            em.dma("sync", lambda s, c=c: s.dma_start(
                out=vpad[c * 128:(c + 1) * 128, 0:32], in_=zpad[:, :]),
                reads=["zpad"], writes=[("vpad0", c)], group="vp0")

        Ssb = ctx.sb("Ssb", [128, H * nkc * DV], F32)
        Sbf = ctx.sb("Sbf", [128, H * nkc * DV], BF16)
        em.op("gpsimd", lambda g: g.memset(Ssb[:, :], 0.0), writes=["S"])
        em.op("gpsimd", lambda g: g.memset(Sbf[:, :], 0.0), writes=["Sb"])
        scr_names = ["A", "B", "A2", "B2", "A4", "B4", "A8", "B8", "A16", "B16",
                     "P1t", "P2t", "P2", "P1", "Q1", "Tb", "Tb_t",
                     "T00", "T00t", "T11t", "M1", "T10t"]
        scr = {}
        for nm in ("A", "B"):
            scr[nm] = ctx.sb("scr" + nm, [64, 64], BF16)
        for nm in scr_names:
            if nm not in scr:
                scr[nm] = ctx.sb("scr" + nm, [32, 32], BF16)
        scr["Ad"] = ctx.sb("scrAd", [32, 32], BF16)
        scr["Bd"] = ctx.sb("scrBd", [32, 32], BF16)
        rlo = ctx.sb("rlo", [32, DV], BF16)
        ua_hi = ctx.sb("ua_hi", [128, DV], BF16)
        Zt = ctx.sb("Zt", [64, DV], BF16)
        ua = ctx.sb("ua", [64, DV], BF16)
        ua2 = ctx.sb("ua2", [64, DV], BF16)
        rref = ctx.sb("rref", [64, DV], BF16)
        lt = ctx.sb("lt", [64, 64], BF16)

        def seg_produce(si, bi):
            s0 = si * SEG
            for c in range(nD):
                em.adopt(("xfm", c), [("xln", c)])
                em.dma("sync", lambda sy, c=c, s0=s0: sy.dma_start_transpose(
                    x_fm[c][:, 0:SEG], xscr[s0:s0 + SEG, c * 128:(c + 1) * 128]),
                    reads=[("xscr", s0)], writes=[("xfm", c)], group=f"xt{si}")
            ln_fm(ctx, x_fm, "xfm", SEG, D, x_fm, "xln", banks=(0, 1),
                  tag=f"lnseg{si}")
            for c in range(nD):
                em.op("tensor", lambda t, c=c: t.matmul(
                    ctx.bank[2][0:H, 0:SEG], wb_sb[c][:, :], x_fm[c][:, 0:SEG],
                    start=(c == 0), stop=(c == nD - 1)),
                    reads=[("xln", c), ("Wb", c)], writes=[pb(2)], accum=(c > 0))
            em.op("scalar", lambda s, s0=s0: s.activation(
                beta_fm[0:H, s0:s0 + SEG], ctx.bank[2][0:H, 0:SEG], AF.Sigmoid),
                reads=[pb(2)], writes=["beta_fm"])
            for ttl in range(SEG // 128):
                tt = s0 // 128 + ttl
                for c in range(nD):
                    em.op("tensor", lambda t, c=c, ttl=ttl: t.matmul(
                        ctx.bank[2][:, 0:H], x_fm[c][:, ttl * 128:(ttl + 1) * 128],
                        wb_sb[c][:, :], start=(c == 0), stop=(c == nD - 1)),
                        reads=[("xln", c), ("Wb", c)], writes=[pb(2)], accum=(c > 0))
                em.op("scalar", lambda s, tt=tt: s.activation(
                    beta_tm[:, tt * H:(tt + 1) * H], ctx.bank[2][:, 0:H], AF.Sigmoid),
                    reads=[pb(2)], writes=["beta_tm"])
            for h in range(H):
                em.op("tensor", lambda t, h=h, s0=s0: t.matmul(
                    ctx.bank[2][:, 0:SEG], selh[h][:, :], beta_fm[0:H, s0:s0 + SEG],
                    start=True, stop=True),
                    reads=[("selh", h), "beta_fm"], writes=[pb(2)])
                em.op("vector", lambda v, h=h: v.tensor_copy(
                    bb[h][:, 0:SEG], ctx.bank[2][:, 0:SEG]),
                    reads=[pb(2)], writes=[("bb", h)])
            for nm, w_sb, wreg, dst, dreg, cwnm in (
                    ("q", wq_sb, "Wq", qs[bi], "qs", "cwq"),
                    ("k", wk_sb, "Wk", ks[bi], "ks", "cwk"),
                    ("v", wv_sb, "Wv", vs[bi], "vs", "cwv")):
                for oc in range(nHD):
                    pb_ = 2 + (oc % 2)
                    base = oc * (SEG + 4)
                    for j0 in range(0, SEG, NMAX):
                        w = min(NMAX, SEG - j0)
                        for c in range(nD):
                            em.op("tensor", lambda t, c=c, oc=oc, j0=j0, w=w, w_sb=w_sb, s0=s0, pb_=pb_: t.matmul(
                                ctx.bank[pb_][:, 0:w],
                                w_sb[c][:, oc * 128:(oc + 1) * 128],
                                x_fm[c][:, j0:j0 + w],
                                start=(c == 0), stop=(c == nD - 1)),
                                reads=[("xln", c), (wreg, c)], writes=[pb(pb_)],
                                accum=(c > 0))
                        em.op("vector", lambda v, oc=oc, j0=j0, w=w, base=base, pb_=pb_: v.tensor_copy(
                            pre[0][:, base + 4 + j0: base + 4 + j0 + w],
                            ctx.bank[pb_][:, 0:w]),
                            reads=[pb(pb_)], writes=[("pre", oc)])
                    em.op("vector", lambda v, oc=oc, base=base, nm=nm: v.tensor_copy(
                        pre[0][:, base:base + 4], tails[nm][:, oc * 4:(oc + 1) * 4]),
                        reads=[("tail", nm)], writes=[("pre", oc)])
                    em.op("vector", lambda v, oc=oc, base=base, nm=nm: v.tensor_copy(
                        tails[nm][:, oc * 4:(oc + 1) * 4],
                        pre[0][:, base + SEG:base + SEG + 4]),
                        reads=[("pre", oc)], writes=[("tailn", nm, oc)])
                for oc in range(nHD):
                    base = oc * (SEG + 4)
                    em.op("vector", lambda v, oc=oc, base=base, cwnm=cwnm: v.tensor_scalar_mul(
                        nsq[:, 0:SEG], pre[0][:, base + 1:base + 1 + SEG],
                        cw_sb[cwnm][:, oc * 4:oc * 4 + 1]),
                        reads=[("pre", oc), cwnm], writes=["nsq"])
                    for tap in range(1, 4):
                        em.op("vector", lambda v, oc=oc, base=base, tap=tap, cwnm=cwnm: v.scalar_tensor_tensor(
                            nsq[:, 0:SEG], pre[0][:, base + 1 + tap:base + 1 + tap + SEG],
                            cw_sb[cwnm][:, oc * 4 + tap:oc * 4 + tap + 1],
                            nsq[:, 0:SEG], ALU.mult, ALU.add),
                            reads=[("pre", oc), cwnm, "nsq"], writes=["nsq"])
                    em.op("scalar", lambda s, oc=oc, dst=dst: s.activation(
                        dst[oc][:, 0:SEG], nsq[:, 0:SEG], AF.Silu),
                        reads=["nsq"], writes=[(dreg, oc)])
            oc128 = ctx.ones_col(128)
            for nm, dst, dreg in (("q", qs[bi], "qs"), ("k", ks[bi], "ks")):
                for h in range(H):
                    for kc in range(nkc):
                        oc = h * nkc + kc
                        em.op("scalar", lambda s, oc=oc, dst=dst: s.activation(
                            nsq[:, 0:SEG], dst[oc][:, 0:SEG], AF.Square),
                            reads=[(dreg, oc)], writes=["nsq"])
                        em.op("tensor", lambda t, kc=kc: t.matmul(
                            ctx.bank[4][0:1, 0:SEG], oc128[:, :],
                            nsq[:, 0:SEG], start=(kc == 0), stop=(kc == nkc - 1)),
                            reads=["nsq", ("ones_col", 128)], writes=[pb(4)],
                            accum=(kc > 0))
                    em.op("vector", lambda v: v.tensor_single_scalar(
                        nrow[0:1, 0:SEG], ctx.bank[4][0:1, 0:SEG], 1e-6, ALU.add),
                        reads=[pb(4)], writes=["nrow"])
                    em.op("scalar", lambda s: s.activation(nrow[0:1, 0:SEG],
                                                           nrow[0:1, 0:SEG], AF.Ln),
                          reads=["nrow"], writes=["nrow"])
                    em.op("vector", lambda v: v.tensor_scalar_mul(
                        nrow[0:1, 0:SEG], nrow[0:1, 0:SEG], -0.5),
                        reads=["nrow"], writes=["nrow"])
                    em.op("scalar", lambda s: s.activation(nrow[0:1, 0:SEG],
                                                           nrow[0:1, 0:SEG], AF.Exp),
                          reads=["nrow"], writes=["nrow"])
                    bcast_row(ctx, "nrow", nrow[0:1, 0:SEG], nbc, "nbc", 128, SEG, 4)
                    for kc in range(nkc):
                        oc = h * nkc + kc
                        em.op("vector", lambda v, oc=oc, dst=dst: v.tensor_mul(
                            dst[oc][:, 0:SEG], dst[oc][:, 0:SEG], nbc[:, 0:SEG]),
                            reads=[(dreg, oc), "nbc"], writes=[(dreg, oc)])
            for h in range(H):
                for kc in range(nkc):
                    oc = h * nkc + kc
                    em.op("vector", lambda v, oc=oc, h=h, bi=bi: v.tensor_mul(
                        kbs[bi][oc][:, 0:SEG], ks[bi][oc][:, 0:SEG],
                        bb[h][:, 0:SEG]),
                        reads=[("ks", oc), ("bb", h)], writes=[("kbs", oc)])
            for oc in range(nHD):
                em.dma("sync", lambda s, oc=oc, s0=s0, bi=bi: s.dma_start(
                    out=kscr[oc * 128:(oc + 1) * 128, s0:s0 + SEG],
                    in_=ks[bi][oc][:, 0:SEG]),
                    reads=[("ks", oc)], writes=[("kscr", si, oc)], group=f"kst{si}")
                em.dma("sync", lambda s, oc=oc, s0=s0, bi=bi: s.dma_start(
                    out=vpad[oc * 128:(oc + 1) * 128, 32 + s0:32 + s0 + SEG],
                    in_=vs[bi][oc][:, 0:SEG]),
                    reads=[("vs", oc)], writes=[("vpd", si, oc)], group=f"vst{si}")
            for ttl in range(SEG // 128):
                em.dma("scalar", lambda s, ttl=ttl, s0=s0, bi=bi: s.dma_start_transpose(
                    kt[bi][:, ttl * HD:(ttl + 1) * HD],
                    kscr[0:HD, s0 + ttl * 128:s0 + (ttl + 1) * 128]),
                    reads=[("kscr", si, oc2) for oc2 in range(nHD)],
                    writes=[("kt", ttl)], group=f"ktr{si}")
                em.dma("scalar", lambda s, ttl=ttl, s0=s0, bi=bi: s.dma_start_transpose(
                    vt[bi][:, ttl * HD:(ttl + 1) * HD],
                    vpad[0:HD, 32 + s0 + ttl * 128:32 + s0 + (ttl + 1) * 128]),
                    reads=[("vpd", si, oc2) for oc2 in range(nHD)],
                    writes=[("vt", ttl)], group=f"vtr{si}")

        def scan_seg(si, bi):
            assert CH == 64
            s0 = si * SEG
            I32 = ident
            for ncl in range(npc):
                t0 = ncl * CH
                ttl = t0 // 128
                o64 = t0 % 128
                for h in range(H):
                    hb = h * nkc
                    sbase = h * nkc * DV
                    for dst, l_list, r_list, msk, mt in (
                            ("A", kbs[bi], ks[bi], "maskA", maskA),
                            ("B", ks[bi], kbs[bi], "maskB", maskB)):
                        for kc in range(nkc):
                            em.op("tensor", lambda t, kc=kc, hb=hb, t0=t0, l_list=l_list, r_list=r_list: t.matmul(
                                ctx.bank[2][0:CH, 0:CH],
                                l_list[hb + kc][:, t0:t0 + CH],
                                r_list[hb + kc][:, t0:t0 + CH],
                                start=(kc == 0), stop=(kc == nkc - 1)),
                                reads=[("ks", hb + kc), ("kbs", hb + kc)],
                                writes=[pb(2)], accum=(kc > 0))
                        em.op("vector", lambda v, dst=dst, mt=mt: v.tensor_mul(
                            scr[dst][0:CH, 0:CH], ctx.bank[2][0:CH, 0:CH],
                            mt[0:CH, 0:CH]),
                            reads=[pb(2), msk], writes=[("scr", dst)])

                    def neu32(r0, want_T):
                        if r0 == 0:
                            Ab = scr["A"][0:32, 0:32]
                            Bb = scr["B"][0:32, 0:32]
                            ra, rb = ("scr", "A"), ("scr", "B")
                        else:
                            em.op("vector", lambda v, r0=r0: v.tensor_copy(
                                scr["Ad"][0:32, 0:32], scr["A"][r0:r0 + 32, r0:r0 + 32]),
                                reads=[("scr", "A")], writes=[("scr", "Ad")])
                            em.op("vector", lambda v, r0=r0: v.tensor_copy(
                                scr["Bd"][0:32, 0:32], scr["B"][r0:r0 + 32, r0:r0 + 32]),
                                reads=[("scr", "B")], writes=[("scr", "Bd")])
                            Ab = scr["Ad"][0:32, 0:32]
                            Bb = scr["Bd"][0:32, 0:32]
                            ra, rb = ("scr", "Ad"), ("scr", "Bd")
                        names = [("A2", "B2"), ("A4", "B4"), ("A8", "B8"),
                                 ("A16", "B16")]
                        prev_ra, prev_rb = ra, rb
                        cur_a, cur_b = Ab, Bb
                        for (na, nb) in names:
                            em.op("tensor", lambda t, cur_a=cur_a, cur_b=cur_b: t.matmul(
                                ctx.bank[3][0:32, 0:32], cur_b, cur_a,
                                start=True, stop=True),
                                reads=[prev_ra, prev_rb], writes=[pb(3)])
                            em.op("vector", lambda v, na=na: v.tensor_copy(
                                scr[na][0:32, 0:32], ctx.bank[3][0:32, 0:32]),
                                reads=[pb(3)], writes=[("scr", na)])
                            em.op("tensor", lambda t, cur_a=cur_a, cur_b=cur_b: t.matmul(
                                ctx.bank[2][0:32, 0:32], cur_a, cur_b,
                                start=True, stop=True),
                                reads=[prev_ra, prev_rb], writes=[pb(2)])
                            em.op("vector", lambda v, nb=nb: v.tensor_copy(
                                scr[nb][0:32, 0:32], ctx.bank[2][0:32, 0:32]),
                                reads=[pb(2)], writes=[("scr", nb)])
                            cur_a, cur_b = scr[na][0:32, 0:32], scr[nb][0:32, 0:32]
                            prev_ra, prev_rb = ("scr", na), ("scr", nb)

                        def prod(dst, lhsT_ap, rhs_ap, reads, adds, bank=3):
                            em.op("tensor", lambda t, lhsT_ap=lhsT_ap, rhs_ap=rhs_ap, bank=bank: t.matmul(
                                ctx.bank[bank][0:32, 0:32], lhsT_ap, rhs_ap,
                                start=True, stop=True),
                                reads=reads, writes=[pb(bank)])
                            first = True
                            for ap, reg in adds:
                                if first:
                                    em.op("vector", lambda v, dst=dst, ap=ap, bank=bank: v.tensor_add(
                                        scr[dst][0:32, 0:32], ctx.bank[bank][0:32, 0:32], ap),
                                        reads=[pb(bank), reg], writes=[("scr", dst)])
                                    first = False
                                else:
                                    em.op("vector", lambda v, dst=dst, ap=ap: v.tensor_add(
                                        scr[dst][0:32, 0:32], scr[dst][0:32, 0:32], ap),
                                        reads=[("scr", dst), reg], writes=[("scr", dst)])
                            if first:
                                em.op("vector", lambda v, dst=dst, bank=bank: v.tensor_copy(
                                    scr[dst][0:32, 0:32], ctx.bank[bank][0:32, 0:32]),
                                    reads=[pb(bank)], writes=[("scr", dst)])

                        A2, B2 = scr["A2"][0:32, 0:32], scr["B2"][0:32, 0:32]
                        A4, B4 = scr["A4"][0:32, 0:32], scr["B4"][0:32, 0:32]
                        A8, B8 = scr["A8"][0:32, 0:32], scr["B8"][0:32, 0:32]
                        A16 = scr["A16"][0:32, 0:32]
                        Ieye = I32[0:32, 0:32]
                        prod("P1t", A2, Bb, [("scr", "A2"), rb],
                             [(Bb, rb), (B2, ("scr", "B2")),
                              (Ieye, ("ident", 128))])
                        prod("P2t", A8, B4, [("scr", "A8"), ("scr", "B4")],
                             [(B4, ("scr", "B4")), (B8, ("scr", "B8")),
                              (Ieye, ("ident", 128))], bank=2)
                        prod("P2", B4, A8, [("scr", "B4"), ("scr", "A8")],
                             [(A4, ("scr", "A4")), (A8, ("scr", "A8")),
                              (Ieye, ("ident", 128))])
                        prod("Q1", scr["P2t"][0:32, 0:32], A16,
                             [("scr", "P2t"), ("scr", "A16")],
                             [(scr["P2"][0:32, 0:32], ("scr", "P2"))], bank=2)
                        prod("Tb_t", scr["Q1"][0:32, 0:32], scr["P1t"][0:32, 0:32],
                             [("scr", "Q1"), ("scr", "P1t")], [])
                        if want_T:
                            prod("P1", Bb, A2, [rb, ("scr", "A2")],
                                 [(Ab, ra), (A2, ("scr", "A2")),
                                  (Ieye, ("ident", 128))], bank=2)
                            prod("Tb", scr["P1t"][0:32, 0:32], scr["Q1"][0:32, 0:32],
                                 [("scr", "P1t"), ("scr", "Q1")], [])

                    neu32(0, True)
                    em.op("vector", lambda v: v.tensor_copy(scr["T00"][0:32, 0:32],
                                                            scr["Tb"][0:32, 0:32]),
                          reads=[("scr", "Tb")], writes=[("scr", "T00")])
                    em.op("vector", lambda v: v.tensor_copy(scr["T00t"][0:32, 0:32],
                                                            scr["Tb_t"][0:32, 0:32]),
                          reads=[("scr", "Tb_t")], writes=[("scr", "T00t")])
                    neu32(32, False)
                    em.op("vector", lambda v: v.tensor_copy(scr["T11t"][0:32, 0:32],
                                                            scr["Tb_t"][0:32, 0:32]),
                          reads=[("scr", "Tb_t")], writes=[("scr", "T11t")])

                    em.op("tensor", lambda t: t.matmul(
                        ctx.bank[3][0:32, 0:32], scr["B"][0:32, 32:64],
                        scr["T00"][0:32, 0:32], start=True, stop=True),
                        reads=[("scr", "B"), ("scr", "T00")], writes=[pb(3)])
                    em.op("vector", lambda v: v.tensor_copy(scr["M1"][0:32, 0:32],
                                                            ctx.bank[3][0:32, 0:32]),
                          reads=[pb(3)], writes=[("scr", "M1")])
                    em.op("tensor", lambda t: t.matmul(
                        ctx.bank[2][0:32, 0:32], scr["M1"][0:32, 0:32],
                        scr["T11t"][0:32, 0:32], start=True, stop=True),
                        reads=[("scr", "M1"), ("scr", "T11t")], writes=[pb(2)])
                    em.op("vector", lambda v: v.tensor_copy(scr["T10t"][0:32, 0:32],
                                                            ctx.bank[2][0:32, 0:32]),
                          reads=[pb(2)], writes=[("scr", "T10t")])

                    for kc in range(nkc):
                        em.op("tensor", lambda t, kc=kc, hb=hb, t0=t0, sbase=sbase: t.matmul(
                            ctx.bank[4][0:CH, 0:DV], kbs[bi][hb + kc][:, t0:t0 + CH],
                            Sbf[:, sbase + kc * DV: sbase + (kc + 1) * DV],
                            start=(kc == 0), stop=(kc == nkc - 1)),
                            reads=[("kbs", hb + kc), "Sb"], writes=[pb(4)],
                            accum=(kc > 0))
                    em.op("vector", lambda v, h=h, ttl=ttl, t0=t0, o64=o64: v.scalar_tensor_tensor(
                        Zt[0:CH, 0:DV],
                        vt[bi][o64:o64 + CH, ttl * HD + h * DV: ttl * HD + (h + 1) * DV],
                        beta_tm[o64:o64 + CH,
                                (s0 + t0) // 128 * H + h: (s0 + t0) // 128 * H + h + 1],
                        ctx.bank[4][0:CH, 0:DV], ALU.mult, ALU.subtract),
                        reads=[("vt", ttl), "beta_tm", pb(4)], writes=["Zt"])

                    def applyT(rhs_tile, rhs_reg, out_tile, out_reg):
                        em.op("vector", lambda v: v.tensor_copy(
                            rlo[0:32, 0:DV], rhs_tile[32:64, 0:DV]),
                            reads=[rhs_reg], writes=["rlo"])
                        em.op("tensor", lambda t: t.matmul(
                            ctx.bank[5][0:32, 0:DV], scr["T00t"][0:32, 0:32],
                            rhs_tile[0:32, 0:DV], start=True, stop=True),
                            reads=[("scr", "T00t"), rhs_reg], writes=[pb(5)])
                        em.op("vector", lambda v: v.tensor_copy(
                            out_tile[0:32, 0:DV], ctx.bank[5][0:32, 0:DV]),
                            reads=[pb(5)], writes=[out_reg])
                        em.op("tensor", lambda t: t.matmul(
                            ctx.bank[5][0:32, 0:DV], scr["T10t"][0:32, 0:32],
                            rhs_tile[0:32, 0:DV], start=True, stop=False),
                            reads=[("scr", "T10t"), rhs_reg], writes=[pb(5)])
                        em.op("tensor", lambda t: t.matmul(
                            ctx.bank[5][0:32, 0:DV], scr["T11t"][0:32, 0:32],
                            rlo[0:32, 0:DV], start=False, stop=True),
                            reads=[("scr", "T11t"), "rlo"], writes=[pb(5)],
                            accum=True)
                        em.op("vector", lambda v: v.tensor_copy(
                            out_tile[32:64, 0:DV], ctx.bank[5][0:32, 0:DV]),
                            reads=[pb(5)], writes=[out_reg])

                    applyT(Zt, "Zt", ua, "ua")
                    em.op("tensor", lambda t: t.matmul(
                        ctx.bank[4][0:CH, 0:DV], scr["B"][0:CH, 0:CH], ua[0:CH, 0:DV],
                        start=True, stop=True),
                        reads=[("scr", "B"), "ua"], writes=[pb(4)])
                    em.op("vector", lambda v: v.scalar_tensor_tensor(
                        rref[0:CH, 0:DV], ua[0:CH, 0:DV], -1.0, ctx.bank[4][0:CH, 0:DV],
                        ALU.mult, ALU.add),
                        reads=["ua", pb(4)], writes=["rref"])
                    em.op("vector", lambda v: v.tensor_add(
                        rref[0:CH, 0:DV], rref[0:CH, 0:DV], Zt[0:CH, 0:DV]),
                        reads=["rref", "Zt"], writes=["rref"])
                    applyT(rref, "rref", ua2, "ua2")
                    em.op("vector", lambda v: v.tensor_add(
                        ua[0:CH, 0:DV], ua[0:CH, 0:DV], ua2[0:CH, 0:DV]),
                        reads=["ua", "ua2"], writes=["ua"])

                    for kc in range(nkc):
                        em.op("tensor", lambda t, kc=kc, hb=hb, t0=t0: t.matmul(
                            ctx.bank[2][0:CH, 0:CH], ks[bi][hb + kc][:, t0:t0 + CH],
                            qs[bi][hb + kc][:, t0:t0 + CH],
                            start=(kc == 0), stop=(kc == nkc - 1)),
                            reads=[("ks", hb + kc), ("qs", hb + kc)],
                            writes=[pb(2)], accum=(kc > 0))
                    em.op("vector", lambda v: v.tensor_mul(
                        lt[0:CH, 0:CH], ctx.bank[2][0:CH, 0:CH], maskL[0:CH, 0:CH]),
                        reads=[pb(2), "maskL"], writes=["lt"])
                    for dc in range(nkc):
                        for kc in range(nkc):
                            em.op("tensor", lambda t, kc=kc, dc=dc, hb=hb, t0=t0, sbase=sbase: t.matmul(
                                ctx.bank[3][:, 0:CH],
                                Sbf[:, sbase + kc * DV + dc * 128: sbase + kc * DV + (dc + 1) * 128],
                                qs[bi][hb + kc][:, t0:t0 + CH],
                                start=(kc == 0), stop=False),
                                reads=["Sb", ("qs", hb + kc)], writes=[pb(3)],
                                accum=(kc > 0))
                        em.op("tensor", lambda t, dc=dc: t.matmul(
                            ctx.bank[3][:, 0:CH], ua[0:CH, dc * 128:(dc + 1) * 128],
                            lt[0:CH, 0:CH], start=False, stop=True),
                            reads=["ua", "lt"], writes=[pb(3)], accum=True)
                        em.op("vector", lambda v, dc=dc, h=h, t0=t0: v.tensor_copy(
                            ods[bi][:, (h * nkc + dc) * SEG + t0: (h * nkc + dc) * SEG + t0 + CH],
                            ctx.bank[3][:, 0:CH]),
                            reads=[pb(3)], writes=[("ods", h, dc)])
                    if o64 == 64:
                        em.op("vector", lambda v: v.tensor_copy(
                            ua_hi[64:128, 0:DV], ua[0:CH, 0:DV]),
                            reads=["ua"], writes=["ua_hi"])
                    for kc in range(nkc):
                        ua_ap = ua[0:CH, 0:DV] if o64 == 0 else ua_hi[64:128, 0:DV]
                        ua_rg = "ua" if o64 == 0 else "ua_hi"
                        em.op("tensor", lambda t, kc=kc, h=h, ttl=ttl, o64=o64, ua_ap=ua_ap: t.matmul(
                            ctx.bank[6][:, 0:DV],
                            kt[bi][o64:o64 + CH,
                                   ttl * HD + h * DK + kc * 128: ttl * HD + h * DK + (kc + 1) * 128],
                            ua_ap, start=True, stop=True),
                            reads=[("kt", ttl), ua_rg], writes=[pb(6)])
                        em.op("vector", lambda v, kc=kc, sbase=sbase: v.tensor_add(
                            Ssb[:, sbase + kc * DV: sbase + (kc + 1) * DV],
                            Ssb[:, sbase + kc * DV: sbase + (kc + 1) * DV],
                            ctx.bank[6][:, 0:DV]),
                            reads=["S", pb(6)], writes=["S"])
                        em.op("vector", lambda v, kc=kc, sbase=sbase: v.tensor_copy(
                            Sbf[:, sbase + kc * DV: sbase + (kc + 1) * DV],
                            Ssb[:, sbase + kc * DV: sbase + (kc + 1) * DV]),
                            reads=["S"], writes=["Sb"])
            for oc in range(nHD):
                em.dma("sync", lambda s, oc=oc, s0=s0, bi=bi: s.dma_start(
                    out=delta[oc * 128:(oc + 1) * 128, s0:s0 + SEG],
                    in_=ods[bi][:, oc * SEG:(oc + 1) * SEG]),
                    reads=[("ods", oc // nkc, oc % nkc)], writes=[("dst", si, oc)],
                    group=f"dst{si}")

        for si in range(nseg):
            bi = 0
            seg_produce(si, bi)
            scan_seg(si, bi)

        em.final_gate("sync", [("dst", si, oc) for si in range(nseg)
                               for oc in range(nHD)]
                      + [("vpd", si, oc) for si in range(nseg) for oc in range(nHD)]
                      + [("vpad0", c) for c in range(nHD)])
        with nc.Block() as block:
            em.finalize(block)
    return nc


def build_L1C2(SH, D, H, DV, FH=2048):
    HD = H * DV
    GI = D + 3 * HD
    nD = D // 128
    nHD = HD // 128
    nkc = DV // 128
    nGI = GI // 128
    nFH = FH // 128
    H4 = H * 4
    ntt = SH // 128
    nc = bass.Bass("TRN2", target_bir_lowering=False)
    xT = nc.dram_tensor("xT", [D, SH], F32, kind="ExternalInput")
    deltao = nc.dram_tensor("deltao", [HD, SH], BF16, kind="ExternalInput")
    vwin = nc.dram_tensor("vwin", [HD, SH + 30], BF16, kind="ExternalInput")
    fsw = nc.dram_tensor("fsw", [128, nHD * 3], F32, kind="ExternalInput")
    flw = nc.dram_tensor("flw", [128, nHD * 31], F32, kind="ExternalInput")
    Wf1 = nc.dram_tensor("Wf1", [GI, FH], F32, kind="ExternalInput")
    bf1c = nc.dram_tensor("bf1c", [128, nFH], F32, kind="ExternalInput")
    Wf2 = nc.dram_tensor("Wf2", [FH, H4], F32, kind="ExternalInput")
    bf2r = nc.dram_tensor("bf2r", [1, H4], F32, kind="ExternalInput")
    ltau = nc.dram_tensor("ltau", [1, H], F32, kind="ExternalInput")
    nwc = nc.dram_tensor("nwc", [128, nHD], F32, kind="ExternalInput")
    Wo = nc.dram_tensor("Wo", [HD, D], F32, kind="ExternalInput")
    Rout = nc.dram_tensor("Rout", [D, SH], F32, kind="ExternalOutput")

    from contextlib import ExitStack
    with ExitStack() as stack:
        em = Emitter(nc)
        ctx = Ctx(nc, em, stack, n_f32_banks=6, transpose_bank=True)
        xbl = [ctx.sb(f"xbl{c}", [128, SH], BF16) for c in range(nD)]
        dl = [ctx.sb(f"dl{c}", [128, SH], BF16) for c in range(nHD)]
        vwc = ctx.sb("vwc", [128, SH + 30], BF16)
        vws = ctx.sb("vws", [128, NMAX], BF16)
        fs = [ctx.sb(f"fs{c}", [128, SH], BF16) for c in range(nHD)]
        fl = [ctx.sb(f"fl{c}", [128, SH], BF16) for c in range(nHD)]
        o_fm = fs
        bo_fm = dl[:nD]
        hid = [ctx.sb(f"hid{c}", [128, SH], BF16) for c in range(2)]
        w1f = [ctx.sb(f"w1f{c}", [128, NMAX], BF16) for c in range(nGI)]
        wf2_sb = [ctx.sb(f"wf2{c}", [128, H4], BF16) for c in range(nFH)]
        wo_sb = [ctx.sb(f"wo{c}", [128, 128], BF16) for c in range(nHD)]
        logacc = ctx.sb("logacc", [128, ntt * H4], F32)
        coeft = ctx.sb("coeft", [128, H4], BF16)
        coeff_fm = ctx.sb("coeffm", [H4, SH], BF16)
        fswb = ctx.sb("fswb", [128, nHD * 3], F32)
        flwb = ctx.sb("flwb", [128, nHD * 31], F32)
        bf1_sb = ctx.sb("bf1s", [128, nFH], F32)
        bf2row = ctx.sb("bf2rw", [1, H4], F32)
        bf2rep = ctx.sb("bf2rep", [128, H4], F32)
        ltrow = ctx.sb("ltrow", [1, H], F32)
        rt16 = ctx.sb("rt16", [1, H4], F32)
        rtrep = ctx.sb("rtrep", [128, H4], F32)
        nw_sb = ctx.sb("nws", [128, nHD], F32)
        nsq = ctx.sb("nsq2", [128, max(SH // 2, NMAX)], F32)
        nsqb = ctx.sb("nsqb2", [128, NMAX], BF16)
        rc = ctx.sb("rc", [128, SH // 2], F32)
        lg = ctx.sb("lg", [128, H4], F32)
        c4a = ctx.sb("c4a", [128, H], F32)
        c4b = ctx.sb("c4b", [128, H], F32)
        c3m = ctx.sb("c3m", [128, 1], F32)
        srow2 = ctx.sb("srow2", [1, NMAX], F32)
        rbc = ctx.sb("rbc", [128, NMAX], BF16)
        cbt = ctx.sb("cbt", [128, NMAX], BF16)
        ident = ctx.identity(128)

        for nm, dram, sb in (("fsw", fsw, fswb), ("flw", flw, flwb),
                             ("bf1c", bf1c, bf1_sb), ("bf2r", bf2r, bf2row),
                             ("ltau", ltau, ltrow), ("nwc", nwc, nw_sb)):
            em.dma("gpsimd", lambda g, dram=dram, sb=sb: g.dma_start(
                out=sb[:, :], in_=dram[:, :]),
                reads=[nm + "_hbm"], writes=[nm], group="sm")
        for c in range(nFH):
            em.dma("gpsimd", lambda g, c=c: g.dma_start(
                out=wf2_sb[c][:, :], in_=Wf2[c * 128:(c + 1) * 128, :]),
                reads=["Wf2_hbm"], writes=[("wf2", c)], group="sm2")
        for c in range(nHD):
            em.dma("sync", lambda s, c=c: s.dma_start(
                out=dl[c][:, :], in_=deltao[c * 128:(c + 1) * 128, :]),
                reads=["deltao_hbm"], writes=[("dl", c)], group="ld")

        for c in range(nD):
            em.dma("gpsimd", lambda g, c=c: g.dma_start(
                out=xbl[c][:, :], in_=xT[c * 128:(c + 1) * 128, :]),
                reads=["xT_hbm"], writes=[("xbl", c)], group="ldx")
        ln_fm(ctx, xbl, "xbl", SH, D, xbl, "xln", banks=(0, 1))

        FHW = SH // 2
        for c in range(nHD):
            em.dma("sync", lambda s, c=c: s.dma_start(
                out=vwc[:, :], in_=vwin[c * 128:(c + 1) * 128, :]),
                reads=["vwin_hbm"], writes=["vwc"], group=f"vw{c}")
            for taps, wsb, wreg, dst, dreg, off in ((3, fswb, "fsw", fs, "fs", 28),
                                                    (31, flwb, "flw", fl, "fl", 0)):
                for hh in range(2):
                    h0 = hh * FHW
                    em.op("vector", lambda v, c=c, wsb=wsb, taps=taps, off=off, h0=h0: v.tensor_scalar_mul(
                        nsq[:, 0:FHW], vwc[:, off + h0:off + h0 + FHW],
                        wsb[:, c * taps:c * taps + 1]),
                        reads=["vwc", wreg], writes=["nsq2"])
                    for tap in range(1, taps):
                        em.op("vector", lambda v, c=c, wsb=wsb, tap=tap, taps=taps, off=off, h0=h0: v.scalar_tensor_tensor(
                            nsq[:, 0:FHW], vwc[:, off + tap + h0:off + tap + h0 + FHW],
                            wsb[:, c * taps + tap:c * taps + tap + 1],
                            nsq[:, 0:FHW], ALU.mult, ALU.add),
                            reads=["vwc", wreg, "nsq2"], writes=["nsq2"])
                    em.op("vector", lambda v, c=c, dst=dst, h0=h0: v.tensor_copy(
                        dst[c][:, h0:h0 + FHW], nsq[:, 0:FHW]),
                        reads=["nsq2"], writes=[(dreg, c)])

        srcs = []
        for c in range(nD):
            srcs.append((xbl[c], ("xln", c)))
        for c in range(nHD):
            srcs.append((fs[c], ("fs", c)))
        for c in range(nHD):
            srcs.append((fl[c], ("fl", c)))
        for c in range(nHD):
            srcs.append((dl[c], ("dl", c)))
        em.op("gpsimd", lambda g: g.memset(logacc[:, :], 0.0), writes=["logacc"])
        nmblk = nFH // 4
        for mblk in range(nmblk):
            m0 = mblk * 512
            for kc in range(nGI):
                em.dma("gpsimd", lambda g, kc=kc, m0=m0: g.dma_start(
                    out=w1f[kc][:, :], in_=Wf1[kc * 128:(kc + 1) * 128, m0:m0 + 512]),
                    reads=["Wf1_hbm"], writes=[("w1f", kc)], group=f"wf1_{mblk}_{kc // 8}")
            for mch in range(2):
                for mi in range(2):
                    mc = mch * 2 + mi
                    bk = 2 + mi
                    for j0 in range(0, SH, NMAX):
                        w = min(NMAX, SH - j0)
                        for kc in range(nGI):
                            em.op("tensor", lambda t, kc=kc, mc=mc, j0=j0, w=w, bk=bk: t.matmul(
                                ctx.bank[bk][:, 0:w], w1f[kc][:, mc * 128:(mc + 1) * 128],
                                srcs[kc][0][:, j0:j0 + w],
                                start=(kc == 0), stop=(kc == nGI - 1)),
                                reads=[srcs[kc][1], ("w1f", kc)], writes=[pb(bk)],
                                accum=(kc > 0))
                        em.op("scalar", lambda s, mi=mi, j0=j0, w=w, bk=bk, mblk=mblk, mc=mc: s.activation(
                            hid[mi][:, j0:j0 + w], ctx.bank[bk][:, 0:w], AF.Gelu,
                            bias=bf1_sb[:, mblk * 4 + mc:mblk * 4 + mc + 1]),
                            reads=[pb(bk), "bf1c"], writes=[("hid", mi)])
                for tt in range(ntt):
                    for mi in range(2):
                        mc = mch * 2 + mi
                        em.op("tensor", lambda t, tt=tt, mi=mi, mc=mc, mblk=mblk: t.matmul(
                            ctx.bank[4][:, 0:H4], hid[mi][:, tt * 128:(tt + 1) * 128],
                            wf2_sb[mblk * 4 + mc][:, :],
                            start=(mi == 0), stop=(mi == 1)),
                            reads=[("hid", mi), ("wf2", mblk * 4 + mc)], writes=[pb(4)],
                            accum=(mi > 0))
                    em.op("vector", lambda v, tt=tt: v.tensor_add(
                        logacc[:, tt * H4:(tt + 1) * H4], logacc[:, tt * H4:(tt + 1) * H4],
                        ctx.bank[4][:, 0:H4]),
                        reads=["logacc", pb(4)], writes=["logacc"])

        bcast_row(ctx, "bf2r", bf2row, bf2rep, "bf2rep", 128, H4, 5)
        em.op("vector", lambda v: v.tensor_scalar_mul(ltrow[:, :], ltrow[0:1, 0:H],
                                                      -1.0),
              reads=["ltau"], writes=["ltau"])
        em.op("scalar", lambda s: s.activation(ltrow[:, :], ltrow[0:1, 0:H], AF.Exp),
              reads=["ltau"], writes=["ltau"])
        em.op("gpsimd", lambda g: g.memset(rt16[:, :], 1.0), writes=["rt16"])
        for h in range(H):
            em.op("vector", lambda v, h=h: v.tensor_copy(
                rt16[0:1, h * 4:h * 4 + 3],
                bass.AP(ltrow, h, [[H, 1], [0, 3]])),
                reads=["ltau", "rt16"], writes=["rt16"])
        bcast_row(ctx, "rt16", rt16, rtrep, "rtrep", 128, H4, 5)

        selsh = ctx.sb("selsh", [H4, 128], BF16)

        def build_sel(i):
            em.op("gpsimd", lambda g, i=i: g.iota(
                selsh[:, :], [[0, 128]], base=-i, channel_multiplier=1,
                allow_small_or_imprecise_dtypes=True), writes=["selsh"])
            em.op("vector", lambda v: v.tensor_single_scalar(
                selsh[:, :], selsh[:, :], 0.0, ALU.is_equal),
                reads=["selsh"], writes=["selsh"])

        for tt in range(ntt):
            em.op("vector", lambda v, tt=tt: v.tensor_add(
                lg[:, :], logacc[:, tt * H4:(tt + 1) * H4], bf2rep[:, :]),
                reads=["logacc", "bf2rep"], writes=["lg"])
            em.op("vector", lambda v: v.tensor_copy(
                c4a[:, 0:H], bass.AP(lg, 3, [[H4, 128], [4, H]])),
                reads=["lg"], writes=["c4a"])
            em.op("scalar", lambda s: s.activation(c4a[:, :], c4a[:, :], AF.Sigmoid),
                  reads=["c4a"], writes=["c4a"])
            em.op("vector", lambda v: v.tensor_scalar(c4b[:, :], c4a[:, :], 0.09, 0.9,
                                                      ALU.mult, ALU.add),
                  reads=["c4a"], writes=["c4b"])
            em.op("vector", lambda v: v.tensor_mul(c4b[:, :], c4b[:, :], c4a[:, :]),
                  reads=["c4a", "c4b"], writes=["c4b"])
            em.op("vector", lambda v: v.tensor_mul(lg[:, :], lg[:, :], rtrep[:, :]),
                  reads=["lg", "rtrep"], writes=["lg"])
            for h in range(H):
                sl = lambda t=None: None
                em.op("vector", lambda v, h=h: v.reduce_max(
                    c3m[:, 0:1], lg[:, h * 4:h * 4 + 3], axis=AX, negate=True),
                    reads=["lg"], writes=["c3m"])
                em.op("scalar", lambda s, h=h: s.activation(
                    lg[:, h * 4:h * 4 + 3], lg[:, h * 4:h * 4 + 3], AF.Exp,
                    bias=c3m[:, 0:1]),
                    reads=["lg", "c3m"], writes=["lg"])
                em.op("vector", lambda v, h=h: v.reduce_sum(
                    c3m[:, 0:1], lg[:, h * 4:h * 4 + 3], axis=AX),
                    reads=["lg"], writes=["c3m"])
                em.op("vector", lambda v: v.reciprocal(c3m[:, 0:1], c3m[:, 0:1]),
                      reads=["c3m"], writes=["c3m"])
                em.op("vector", lambda v, h=h: v.tensor_scalar(
                    c4a[:, h:h + 1], c4b[:, h:h + 1], -1.0, 1.0, ALU.mult, ALU.add),
                    reads=["c4b"], writes=["c4a"])
                em.op("vector", lambda v, h=h: v.tensor_mul(
                    c3m[:, 0:1], c3m[:, 0:1], c4a[:, h:h + 1]),
                    reads=["c3m", "c4a"], writes=["c3m"])
                em.op("vector", lambda v, h=h: v.tensor_scalar_mul(
                    coeft[:, h * 4:h * 4 + 3], lg[:, h * 4:h * 4 + 3], c3m[:, 0:1]),
                    reads=["lg", "c3m"], writes=["coeft"])
                em.op("vector", lambda v, h=h: v.tensor_copy(
                    coeft[:, h * 4 + 3:h * 4 + 4], c4b[:, h:h + 1]),
                    reads=["c4b"], writes=["coeft"])
            em.op("tensor", lambda t, tt=tt: t.transpose(
                ctx.bankT[0:H4, 0:128], coeft[:, 0:H4], ident[:, :]),
                reads=["coeft", ("ident", 128)], writes=["pbT"])
            em.op("vector", lambda v, tt=tt: v.tensor_copy(
                coeff_fm[:, tt * 128:(tt + 1) * 128], ctx.bankT[0:H4, 0:128]),
                reads=["pbT"], writes=["coeffm"])

        for c in range(nHD):
            em.adopt(("ofm", c), [("fs", c)])
        for h in range(H):
            for ci in range(4):
                i = h * 4 + ci
                build_sel(i)
                for j0 in range(0, SH, NMAX):
                    w = min(NMAX, SH - j0)
                    em.op("tensor", lambda t, i=i, j0=j0, w=w: t.matmul(
                        ctx.bank[5][:, 0:w], selsh[:, :], coeff_fm[:, j0:j0 + w],
                        start=True, stop=True),
                        reads=["selsh", "coeffm"], writes=[pb(5)])
                    em.op("vector", lambda v, j0=j0, w=w: v.tensor_copy(
                        cbt[:, 0:w], ctx.bank[5][:, 0:w]),
                        reads=[pb(5)], writes=["cbt"])
                    for kc in range(nkc):
                        c = h * nkc + kc
                        if ci == 0:
                            em.op("vector", lambda v, c=c, j0=j0, w=w: v.tensor_mul(
                                o_fm[c][:, j0:j0 + w], fs[c][:, j0:j0 + w], cbt[:, 0:w]),
                                reads=[("fs", c), "cbt"], writes=[("ofm", c)])
                        else:
                            sreg = (("fl", c) if ci == 1 else ("dl", c)) if ci < 3 else ("vw", c)
                            if ci == 1:
                                ap = fl[c][:, j0:j0 + w]
                            elif ci == 2:
                                ap = dl[c][:, j0:j0 + w]
                            else:
                                em.dma("sync", lambda s, c=c, j0=j0, w=w: s.dma_start(
                                    out=vws[:, 0:w],
                                    in_=vwin[c * 128:(c + 1) * 128, 30 + j0:30 + j0 + w]),
                                    reads=["vwin_hbm"], writes=["vws"],
                                    group=f"vws{c}_{j0}")
                                ap = vws[:, 0:w]
                                sreg = "vws"
                            em.op("vector", lambda v, c=c, ap=ap, w=w: v.tensor_mul(
                                nsq[:, 0:w], ap, cbt[:, 0:w]),
                                reads=[sreg, "cbt"], writes=["nsq2"])
                            em.op("vector", lambda v, c=c, j0=j0, w=w: v.tensor_add(
                                o_fm[c][:, j0:j0 + w], o_fm[c][:, j0:j0 + w], nsq[:, 0:w]),
                                reads=[("ofm", c), "nsq2"], writes=[("ofm", c)])

        ones = ctx.ones_col(128)
        or128f = ctx.ones_row(128, F32)
        for h in range(H):
            for j0 in range(0, SH, NMAX):
                w = min(NMAX, SH - j0)
                for kc in range(nkc):
                    c = h * nkc + kc
                    em.op("scalar", lambda s, c=c, j0=j0, w=w: s.activation(
                        nsqb[:, 0:w], o_fm[c][:, j0:j0 + w], AF.Square),
                        reads=[("ofm", c)], writes=["nsqb2"])
                    em.op("tensor", lambda t, kc=kc, j0=j0, w=w: t.matmul(
                        ctx.bank[2][0:1, 0:w], ones[:, :], nsqb[:, 0:w],
                        start=(kc == 0), stop=(kc == nkc - 1)),
                        reads=["nsqb2", ("ones_col", 128)], writes=[pb(2)],
                        accum=(kc > 0))
                em.op("vector", lambda v, j0=j0, w=w: v.tensor_scalar(
                    srow2[0:1, 0:w], ctx.bank[2][0:1, 0:w], 1.0 / DV, 1e-5,
                    ALU.mult, ALU.add),
                    reads=[pb(2)], writes=["srow2"])
                em.op("scalar", lambda s, w=w: s.activation(srow2[0:1, 0:w],
                                                            srow2[0:1, 0:w], AF.Ln),
                      reads=["srow2"], writes=["srow2"])
                em.op("vector", lambda v, w=w: v.tensor_scalar_mul(
                    srow2[0:1, 0:w], srow2[0:1, 0:w], -0.5),
                    reads=["srow2"], writes=["srow2"])
                em.op("scalar", lambda s, w=w: s.activation(srow2[0:1, 0:w],
                                                            srow2[0:1, 0:w], AF.Exp),
                      reads=["srow2"], writes=["srow2"])
                em.op("tensor", lambda t, j0=j0, w=w: t.matmul(
                    ctx.bank[3][:, 0:w], or128f[:, :],
                    srow2[0:1, 0:w], start=True, stop=True),
                    reads=["srow2", ("ones_row", 128, str(F32))], writes=[pb(3)])
                em.op("vector", lambda v, j0=j0, w=w: v.tensor_copy(
                    rbc[:, 0:w], ctx.bank[3][:, 0:w]),
                    reads=[pb(3)], writes=["rbc"])
                for kc in range(nkc):
                    c = h * nkc + kc
                    em.op("vector", lambda v, c=c, j0=j0, w=w: v.scalar_tensor_tensor(
                        o_fm[c][:, j0:j0 + w], o_fm[c][:, j0:j0 + w], nw_sb[:, c:c + 1],
                        rbc[:, 0:w], ALU.mult, ALU.mult),
                        reads=[("ofm", c), "nwc", "rbc"], writes=[("ofm", c)])

        for e in range(nD):
            em.adopt(("bofm", e), [("dl", e)])
            for c in range(nHD):
                em.dma("gpsimd", lambda g, c=c, e=e: g.dma_start(
                    out=wo_sb[c][:, :], in_=Wo[c * 128:(c + 1) * 128,
                                               e * 128:(e + 1) * 128]),
                    reads=["Wo_hbm"], writes=[("wo", c)], group=f"woe{e}")
            for j0 in range(0, SH, NMAX):
                w = min(NMAX, SH - j0)
                for c in range(nHD):
                    em.op("tensor", lambda t, e=e, c=c, j0=j0, w=w: t.matmul(
                        ctx.bank[2][:, 0:w], wo_sb[c][:, :],
                        o_fm[c][:, j0:j0 + w],
                        start=(c == 0), stop=(c == nHD - 1)),
                        reads=[("wo", c), ("ofm", c)], writes=[pb(2)], accum=(c > 0))
                em.op("vector", lambda v, e=e, j0=j0, w=w: v.tensor_copy(
                    bo_fm[e][:, j0:j0 + w], ctx.bank[2][:, 0:w]),
                    reads=[pb(2)], writes=[("bofm", e)])
        ln_fm(ctx, bo_fm, "bofm", SH, D, bo_fm, "boln", banks=(0, 1))
        HW2c = SH // 2
        for c in range(nD):
            for hh in range(2):
                em.dma("sync", lambda s, c=c, hh=hh: s.dma_start(
                    out=rc[:, :], in_=xT[c * 128:(c + 1) * 128, hh * HW2c:(hh + 1) * HW2c]),
                    reads=["xT_hbm"], writes=["rc"], group=f"rld{c}_{hh}")
                em.op("vector", lambda v, c=c, hh=hh: v.tensor_add(
                    rc[:, :], rc[:, :], bo_fm[c][:, hh * HW2c:(hh + 1) * HW2c]),
                    reads=["rc", ("boln", c)], writes=["rc"])
                em.dma("sync", lambda s, c=c, hh=hh: s.dma_start(
                    out=Rout[c * 128:(c + 1) * 128, hh * HW2c:(hh + 1) * HW2c], in_=rc[:, :]),
                    reads=["rc"], writes=[("Rout", c)], group=f"rst{c}_{hh}")
        em.final_gate("sync", [("Rout", c) for c in range(nD)])
        with nc.Block() as block:
            em.finalize(block)
    return nc


_B, _S, _D, _SH = 4, 4096, 1024, 2048
_NCS = {}


def _get_nc(name, builder, *args):
    if name not in _NCS:
        _NCS[name] = builder(*args)
    return _NCS[name]


def _cols(v):
    return np.ascontiguousarray(np.asarray(v, np.float32).reshape(-1, 128).T)


def _colpack(w, ncnk):
    w = np.asarray(w, np.float32)
    k = w.shape[1]
    return np.ascontiguousarray(
        w.reshape(ncnk, 128, k).transpose(1, 0, 2).reshape(128, ncnk * k))


def _run(nc, in_maps):
    from concourse.bass_utils import run_bass_kernel_spmd
    return run_bass_kernel_spmd(nc, in_maps, core_ids=list(range(8)))


def kernel(**inp):
    B, S, D, SH = _B, _S, _D, _SH
    f = lambda k: np.ascontiguousarray(np.asarray(inp[k], dtype=np.float32))
    x = f("L0_attn_x_BSE")

    nc1 = _get_nc("A1", build_L0A1, S, D, 8, 64)
    wq3 = f("L0_attn_w_q"); wk3 = f("L0_attn_w_k"); wv3 = f("L0_attn_w_v")
    maps = []
    for i in range(8):
        b, hg = i // 2, i % 2
        sl = slice(hg * 8, (hg + 1) * 8)
        maps.append({
            "xfull": x[b],
            "wq": np.ascontiguousarray(wq3[:, sl, :].reshape(D, 512)),
            "wk": np.ascontiguousarray(wk3[:, sl, :].reshape(D, 512)),
            "wv": np.ascontiguousarray(wv3[:, sl, :].reshape(D, 512)),
        })
    r = _run(nc1, maps)
    O0 = [np.asarray(r.results[i]["O"]) for i in range(8)]

    nc2 = _get_nc("A2B", build_L0A2B, SH, D, 4096)
    woT = np.ascontiguousarray(
        f("L0_attn_w_o").transpose(1, 2, 0).reshape(D, D))
    w1T = np.ascontiguousarray(f("L0_ffn_w1").T)
    w2T = np.ascontiguousarray(f("L0_ffn_w2").T)
    b1c = _cols(inp["L0_ffn_b1"])
    b2c = _cols(inp["L0_ffn_b2"])
    maps = []
    for i in range(8):
        b, h = i // 2, i % 2
        rows = slice(h * SH, (h + 1) * SH)
        Oo = np.concatenate([O0[2 * b][:, rows], O0[2 * b + 1][:, rows]], axis=0)
        maps.append({
            "xT": np.ascontiguousarray(x[b, rows].T),
            "Oo": np.ascontiguousarray(Oo),
            "woT": woT, "w1T": w1T, "b1c": b1c, "w2T": w2T, "b2c": b2c,
        })
    r = _run(nc2, maps)
    R1 = [np.asarray(r.results[i]["R1"]) for i in range(8)]

    nc3 = _get_nc("C1", build_L1C1, S, D, 4, 256, 64, 512)
    x2 = [np.concatenate([R1[2 * b].T, R1[2 * b + 1].T], axis=0) for b in range(B)]
    maps = []
    for i in range(8):
        b = i // 2
        maps.append({
            "xfull": np.ascontiguousarray(x2[b]),
            "Wq": f("L1_attn_W_q"), "Wk": f("L1_attn_W_k"), "Wv": f("L1_attn_W_v"),
            "Wb": f("L1_attn_W_b"),
            "cwq": _colpack(inp["L1_attn_conv_w_q"], 8),
            "cwk": _colpack(inp["L1_attn_conv_w_k"], 8),
            "cwv": _colpack(inp["L1_attn_conv_w_v"], 8),
        })
    r = _run(nc3, maps)
    delta = [np.asarray(r.results[i]["delta"]) for i in range(8)]
    vpad = [np.asarray(r.results[i]["vpad"]) for i in range(8)]

    nc4 = _get_nc("C2", build_L1C2, SH, D, 4, 256, 2048)
    fsw = np.asarray(inp["L1_attn_fir_short_w"], np.float32).reshape(1024, 3)
    flw = np.asarray(inp["L1_attn_fir_long_w"], np.float32).reshape(1024, 31)
    nwc = _cols(np.tile(np.asarray(inp["L1_attn_norm_w"], np.float32), 4))
    bf1c = _cols(inp["L1_attn_b_fuse1"])
    maps = []
    for i in range(8):
        b, h = i // 2, i % 2
        c0 = h * SH
        maps.append({
            "xT": R1[i],
            "deltao": np.ascontiguousarray(delta[i][:, c0:c0 + SH]),
            "vwin": np.ascontiguousarray(vpad[i][:, c0 + 2:c0 + 32 + SH]),
            "fsw": _colpack(fsw, 8), "flw": _colpack(flw, 8),
            "Wf1": f("L1_attn_W_fuse1"), "bf1c": bf1c,
            "Wf2": f("L1_attn_W_fuse2"),
            "bf2r": np.asarray(inp["L1_attn_b_fuse2"], np.float32)[None],
            "ltau": np.asarray(inp["L1_attn_others_log_tau"], np.float32)[None],
            "nwc": nwc, "Wo": f("L1_attn_W_o"),
        })
    r = _run(nc4, maps)
    R2 = [np.asarray(r.results[i]["Rout"]) for i in range(8)]

    nc5 = _get_nc("D1", build_L1D1, S, D, 512, 8, 64)
    x3 = [np.concatenate([R2[2 * b].T, R2[2 * b + 1].T], axis=0) for b in range(B)]
    wuq = f("L1_ffn_Wuq"); wuk = f("L1_ffn_Wuk"); wuv = f("L1_ffn_Wuv")
    buq = np.asarray(inp["L1_ffn_buq"], np.float32)
    buk = np.asarray(inp["L1_ffn_buk"], np.float32)
    buv = np.asarray(inp["L1_ffn_buv"], np.float32)
    maps = []
    for i in range(8):
        b, hg = i // 2, i % 2
        cs = slice(hg * 512, (hg + 1) * 512)
        maps.append({
            "xfull": np.ascontiguousarray(x3[b]),
            "wdkv": f("L1_ffn_Wdkv"), "bdkvc": _cols(inp["L1_ffn_bdkv"]),
            "wdq": f("L1_ffn_Wdq"), "bdqc": _cols(inp["L1_ffn_bdq"]),
            "wuq": np.ascontiguousarray(wuq[:, cs]), "buqr": buq[None, cs],
            "wuk": np.ascontiguousarray(wuk[:, cs]), "bukr": buk[None, cs],
            "wuv": np.ascontiguousarray(wuv[:, cs]), "buvr": buv[None, cs],
        })
    r = _run(nc5, maps)
    O1 = [np.asarray(r.results[i]["O"]) for i in range(8)]

    nc6 = _get_nc("D2", build_L1D2, SH, D, 4096, 8)
    import jax
    cpu = jax.devices("cpu")[0]
    with jax.default_device(cpu):
        noise = np.asarray(jax.random.normal(
            jax.random.key(42, impl="threefry2x32"), (B, S, 8),
            dtype=jax.numpy.float32))
    W1r = np.ascontiguousarray(f("L1_ffn_W1").reshape(8 * D, 4096))
    W2r = np.ascontiguousarray(f("L1_ffn_W2").reshape(8 * 4096, D))
    b1cs = _cols(inp["L1_ffn_b1"])
    b2cs = _cols(inp["L1_ffn_b2"])
    maps = []
    for i in range(8):
        b, h = i // 2, i % 2
        rows = slice(h * SH, (h + 1) * SH)
        Oo = np.concatenate([O1[2 * b][:, rows], O1[2 * b + 1][:, rows]], axis=0)
        maps.append({
            "xT": R2[i], "Oo": np.ascontiguousarray(Oo),
            "wo": f("L1_ffn_Wo"), "boc": _cols(inp["L1_ffn_bo"]),
            "g1c": _cols(inp["L1_ffn_gamma1"]), "be1c": _cols(inp["L1_ffn_beta1"]),
            "g2c": _cols(inp["L1_ffn_gamma2"]), "be2c": _cols(inp["L1_ffn_beta2"]),
            "wg": f("L1_ffn_Wg"), "bgr": np.asarray(inp["L1_ffn_bg"], np.float32)[None],
            "wn": f("L1_ffn_Wn"), "bnr": np.asarray(inp["L1_ffn_bn"], np.float32)[None],
            "noise": np.ascontiguousarray(noise[b, rows]),
            "W1": W1r, "b1cs": b1cs, "W2": W2r, "b2cs": b2cs,
        })
    r = _run(nc6, maps)
    out = np.empty((B, S, D), np.float32)
    for i in range(8):
        b, h = i // 2, i % 2
        out[b, h * SH:(h + 1) * SH] = np.asarray(r.results[i]["Rout"]).T
    return out
